# version 14 (final)
# baseline (speedup 1.0000x reference)
import numpy as np
import ml_dtypes
import concourse.bass as bass
import concourse.mybir as mybir
from concourse.bass_utils import run_bass_kernel_spmd
from contextlib import ExitStack

F32 = mybir.dt.float32
BF16 = mybir.dt.bfloat16
I32 = mybir.dt.int32
AF = mybir.ActivationFunctionType
ALU = mybir.AluOpType
AX = mybir.AxisListType.X
NPBF = ml_dtypes.bfloat16

NCORE = 8
T = 2048
NT = 16
D = 1024
ND = 8
H = 16
NBUK = 32
DH4 = 4096
NH4 = 32
MAGIC = 12582912.0
QS = 64 ** -0.25
L1SC = 0.125
W1MEAN_N = 4096 * 1024


class Prog:
    def __init__(self, name="prog"):
        self.nc = bass.Bass("TRN2", target_bir_lowering=False)
        self.es = ExitStack()
        self.steps = []
        self._ctx_open = False

    def __enter__(self):
        self.es.__enter__()
        return self

    def sbuf(self, name, shape, dtype):
        return self.es.enter_context(self.nc.sbuf_tensor(name, shape, dtype))

    def psum(self, name, shape, dtype):
        return self.es.enter_context(self.nc.psum_tensor(name, shape, dtype))

    def step(self, eng, fn, dma=False, n_dma=1, relax=0):
        self.steps.append((eng, fn, 16 * n_dma if dma else 1, relax))

    def finish(self):
        SEG = 3600
        ENGS = ("sync", "vector", "scalar", "tensor", "gpsimd")
        steps = self.steps
        n = len(steps)
        cumT = [0] * (n + 1)
        expectG = [0] * (n + 1)
        for si, st in enumerate(steps):
            t_inc = st[2] if st[0] == "tensor" else 0
            g_inc = st[2] if st[0] != "tensor" else 0
            cumT[si + 1] = cumT[si] + t_inc
            expectG[si + 1] = expectG[si] + g_inc
        assert cumT[-1] < 4000, cumT[-1]
        segs = expectG[-1] // SEG + 1
        Gs = [self.es.enter_context(self.nc.semaphore(f"G{k}"))
              for k in range(segs)]
        Tsem = self.es.enter_context(self.nc.semaphore("Tm"))
        segof = [expectG[si] // SEG for si in range(n)]
        locG = [expectG[si] - segof[si] * SEG for si in range(n)]
        runseg = [0] * segs
        cumseg = []
        for si, st in enumerate(steps):
            if st[0] != "tensor":
                runseg[segof[si]] += st[2]
            cumseg.append(tuple(runseg))
        seg_tot = tuple(runseg)

        def g_wait(eng, hw, through):
            need = cumseg[through]
            for k in range(segs):
                if need[k] > hw[k]:
                    eng.wait_ge(Gs[k], need[k])
                    hw[k] = need[k]
        blk = self.es.enter_context(self.nc.Block())

        class DrainProxy:
            def __init__(self, eng):
                object.__setattr__(self, "_e", eng)
                object.__setattr__(self, "_n", 0)

            def __getattr__(self, name):
                f = getattr(self._e, name)
                if not callable(f):
                    return f
                if name == "drain":
                    def wrapd(*a, **k):
                        object.__setattr__(self, "_n", 0)
                        return f(*a, **k)
                    return wrapd
                if name == "wait_ge":
                    return f

                def wrap(*a, **k):
                    if self._n > 0:
                        self._e.drain()
                    object.__setattr__(self, "_n", self._n + 1)
                    return f(*a, **k)
                return wrap

        def make_body(eng_name):
            def body(eng):
                hw = [0] * segs
                hwT = 0
                last = -1
                for si, st in enumerate(steps):
                    se, fn, inc = st[0], st[1], st[2]
                    relax = st[3] if len(st) > 3 else 0
                    if se != eng_name:
                        continue
                    d = si - 1 - relax
                    if d >= 0:
                        if not (last >= 0 and d <= last):
                            pass
                        g_wait(eng, hw, d)
                        if eng_name != "tensor" and cumT[d + 1] > hwT:
                            eng.wait_ge(Tsem, cumT[d + 1])
                            hwT = cumT[d + 1]
                    last = si
                    sem = Tsem if se == "tensor" else Gs[segof[si]]
                    if inc % 16 == 0 and inc > 0:
                        lastins = fn(eng)
                        lastins.then_inc(sem, inc)
                    else:
                        fn(DrainProxy(eng))
                        eng.drain().then_inc(sem, inc)
            return body

        used = {st[0] for st in steps}
        for name in ENGS:
            if name in used:
                getattr(blk, name)(make_body(name))
        @blk.sync
        def _(sync):
            hw = [0] * segs
            g_wait(sync, hw, n - 1)
            if cumT[-1] > 0:
                sync.wait_ge(Tsem, cumT[-1])
        self.es.__exit__(None, None, None)
        return self.nc


def tsc(t, i, w=128):
    return t[:, i * w:(i + 1) * w]



def e_ln(p, x_ap, out_ap, tmp, cola, eps=1e-5):
    def v1(v):
        v.reduce_sum(out=cola[:, 0:1], in_=x_ap, axis=AX, negate=True)
        return v.tensor_scalar_mul(out=cola[:, 0:1], in0=cola[:, 0:1],
                                   scalar1=1.0 / D)
    p.step("vector", v1)
    p.step("scalar", lambda s: s.activation(out=tmp[:], in_=x_ap, func=AF.Square,
                                            bias=cola[:, 0:1]))
    def v2(v):
        v.reduce_sum(out=cola[:, 1:2], in_=tmp[:], axis=AX)
        v.tensor_scalar_mul(out=cola[:, 1:2], in0=cola[:, 1:2], scalar1=1.0 / D)
        return v.tensor_scalar_add(out=cola[:, 1:2], in0=cola[:, 1:2],
                                   scalar1=eps)
    p.step("vector", v2)
    p.step("vector", lambda v: v.reciprocal(out=cola[:, 2:3], in_=cola[:, 1:2]))
    p.step("scalar", lambda s: s.activation(out=cola[:, 2:3], in_=cola[:, 2:3],
                                            func=AF.Sqrt))
    p.step("vector", lambda v: v.tensor_mul(out=cola[:, 3:4], in0=cola[:, 0:1],
                                            in1=cola[:, 2:3]))
    p.step("scalar", lambda s: s.activation(out=out_ap, in_=x_ap, func=AF.Identity,
                                            bias=cola[:, 3:4], scale=cola[:, 2:3]))


def e_quant(p, x_ap, out_bf, tmp_f32, colm, cols, colis):
    def v1(v):
        v.reduce_max(out=colm, in_=x_ap, axis=AX, apply_absolute_value=True)
        v.tensor_scalar_max(out=colm, in0=colm, scalar1=1e-5)
        v.reciprocal(out=cols, in_=colm)
        v.tensor_scalar_mul(out=cols, in0=cols, scalar1=127.0)
        return v.tensor_scalar_mul(out=colis, in0=colm, scalar1=1.0 / 127.0)
    p.step("vector", v1)
    p.step("scalar", lambda s: s.activation(out=tmp_f32, in_=x_ap,
                                            func=AF.Identity,
                                            bias=p._mgc[:, 0:1],
                                            scale=cols))
    def v2(v):
        v.tensor_copy(out=tmp_f32.bitcast(I32), in_=tmp_f32.bitcast(I32))
        return v.tensor_scalar_sub(out=out_bf, in0=tmp_f32, scalar1=MAGIC)
    p.step("vector", v2)


def e_transpose(p, src_bf, dst_fn, ident, ps_ap, n):
    for j in range(n):
        p.step("tensor", lambda t, j=j: t.transpose(
            ps_ap, src_bf[:, j * 128:(j + 1) * 128], ident[:]))
        p.step("vector", lambda v, j=j: v.tensor_copy(out=dst_fn(j), in_=ps_ap))


def e_proj(p, lhsT_sb, w_sb, ps, out_emit, n_chunks=2, nd=ND, ti=None):
    for nch in range(n_chunks):
        def mm(t, nch=nch):
            for kk in range(nd):
                last = t.matmul(ps[:], tsc(lhsT_sb, kk),
                                w_sb[:, kk * n_chunks * 512 + nch * 512:
                                     kk * n_chunks * 512 + nch * 512 + 512],
                                start=(kk == 0), stop=(kk == nd - 1))
            return last
        p.step("tensor", mm)
        out_emit(nch, ps)


def e_softmax_seg(p, psa, psb, out_bf, t1, cola, seg, scale):
    p.step("scalar", lambda s: s.activation(out=t1[:, 0:512], in_=psa[:],
                                            func=AF.Identity, scale=scale))
    p.step("scalar", lambda s: s.activation(out=t1[:, 512:1024], in_=psb[:],
                                            func=AF.Identity, scale=scale))
    p.step("vector", lambda v: v.reduce_max(out=cola[:, 0:1], in_=t1[:], axis=AX,
                                            negate=True))
    p.step("scalar", lambda s: s.activation(out=t1[:], in_=t1[:], func=AF.Exp,
                                            bias=cola[:, 0:1]))
    def v2(v):
        v.reduce_sum(out=seg[:], in_=t1[:].rearrange("p (g k) -> p g k", k=64),
                     axis=AX)
        return v.reciprocal(out=seg[:], in_=seg[:])
    p.step("vector", v2)
    p.step("vector", lambda v: v.tensor_tensor(
        out=out_bf.rearrange("p (g k) -> p g k", k=64),
        in0=t1[:].rearrange("p (g k) -> p g k", k=64),
        in1=seg[:].unsqueeze(2).broadcast_to([128, 16, 64]), op=ALU.mult))


def e_wquant(p, wT_d, wq_sb, ntiles, width, stg, asum, sc1, thr_col, mw_col,
             ones128_f, tmpw, mean_n):
    for j in range(ntiles):
        p.step("sync", lambda s, j=j: s.dma_start(
            out=stg[:, 0:width],
            in_=wT_d[j * 128:(j + 1) * 128, :]), dma=True)
        def v1(v, j=j):
            v.reduce_sum(out=sc1[:, 0:1],
                         in_=stg[:, 0:width],
                         axis=AX, apply_absolute_value=True)
            if j == 0:
                return v.tensor_copy(out=asum[:, 0:1], in_=sc1[:, 0:1])
            return v.tensor_add(out=asum[:, 0:1], in0=asum[:, 0:1],
                                in1=sc1[:, 0:1])
        p.step("vector", v1)
    ps1 = p._ps1
    p.step("tensor", lambda t: t.matmul(ps1[0:1, 0:1], asum[:, 0:1],
                                        ones128_f[:, 0:1], start=True, stop=True))
    def v2(v):
        v.tensor_scalar(out=sc1[0:1, 0:1], in0=ps1[0:1, 0:1],
                        scalar1=1.0 / mean_n, scalar2=1e-5,
                        op0=ALU.mult, op1=ALU.max)
        return v.tensor_scalar_mul(out=sc1[0:1, 1:2], in0=sc1[0:1, 0:1],
                                   scalar1=0.5)
    p.step("vector", v2)
    ones_row = p._ones_row_f
    p.step("tensor", lambda t: t.matmul(ps1[:, 2:4], ones_row[0:1, :],
                                        sc1[0:1, 0:2], start=True, stop=True))
    def v3(v):
        v.tensor_copy(out=mw_col[:, 0:1], in_=ps1[:, 2:3])
        return v.tensor_copy(out=thr_col[:, 0:1], in_=ps1[:, 3:4])
    p.step("vector", v3)
    for j in range(ntiles):
        p.step("sync", lambda s, j=j: s.dma_start(
            out=stg[:, 0:width],
            in_=wT_d[j * 128:(j + 1) * 128, :]), dma=True)
        p.step("scalar", lambda s: s.sign(out=tmpw[:, 0:width],
                                          in_=stg[:, 0:width]))
        p.step("scalar", lambda s: s.activation(out=stg[:, 0:width],
                                                in_=stg[:, 0:width], func=AF.Abs))
        def v4(v, j=j):
            v.tensor_single_scalar(out=stg[:, 0:width],
                                   in_=stg[:, 0:width],
                                   scalar=thr_col[:, 0:1], op=ALU.is_gt)
            v.drain()
            return v.tensor_tensor(out=wq_sb[:, j * width:(j + 1) * width],
                                   in0=tmpw[:, 0:width],
                                   in1=stg[:, 0:width], op=ALU.mult)
        p.step("vector", v4)



def build_l1():
    p = Prog()
    nc = p.nc
    x_d = nc.dram_tensor("x", [T, D], F32, kind="ExternalInput").ap()
    wqT_d = nc.dram_tensor("wqT", [D, D], F32, kind="ExternalInput").ap()
    wkT_d = nc.dram_tensor("wkT", [D, D], F32, kind="ExternalInput").ap()
    wvT_d = nc.dram_tensor("wvT", [D, D], F32, kind="ExternalInput").ap()
    ident_d = nc.dram_tensor("ident", [128, 128], BF16, kind="ExternalInput").ap()
    onesb_d = nc.dram_tensor("onesb", [128, 1], BF16, kind="ExternalInput").ap()
    sq_d = nc.dram_tensor("sq", [T, D], BF16, kind="ExternalOutput").ap()
    sk_d = nc.dram_tensor("sk", [T, D], BF16, kind="ExternalOutput").ap()
    sv_d = nc.dram_tensor("sv", [T, D], BF16, kind="ExternalOutput").ap()
    kvt_d = nc.dram_tensor("kvtot", [64, 1040], F32, kind="ExternalOutput").ap()
    with p.es:
        ident = p.sbuf("s_ident", [128, 128], BF16)
        onesb = p.sbuf("s_onesb", [128, 1], BF16)
        xt = p.sbuf("xt", [128, D], F32)
        xln = p.sbuf("xln", [128, D], F32)
        xlnb = p.sbuf("xlnb", [128, D], BF16)
        xlnT = p.sbuf("xlnT", [128, ND * 128], BF16)
        wq = p.sbuf("wq", [128, ND * D], BF16)
        wk = p.sbuf("wk", [128, ND * D], BF16)
        wv = p.sbuf("wv", [128, ND * D], BF16)
        wstg = p.sbuf("wstg", [128, 2 * D], F32)
        t0 = p.sbuf("t0", [128, D], F32)
        t1 = p.sbuf("t1", [128, D], F32)
        t2 = p.sbuf("t2", [128, D], F32)
        qb = p.sbuf("qb", [128, D], BF16)
        kb = p.sbuf("kb", [128, D], BF16)
        vb = p.sbuf("vb", [128, D], BF16)
        cola = p.sbuf("cola", [128, 4], F32)
        seg = p.sbuf("seg", [128, 16], F32)
        kvtot = p.sbuf("s_kvtot", [64, 1040], F32)
        ps_a = p.psum("ps_a", [128, 512], F32)
        ps_b = p.psum("ps_b", [128, 512], F32)
        ps_t = p.psum("ps_t", [128, 128], BF16)
        ps_ctx = p.psum("ps_ctx", [64, 1024], F32)
        ps_ks = p.psum("ps_ks", [64, 16], F32)

        p.step("sync", lambda s: s.dma_start(out=ident[:], in_=ident_d), dma=True)
        p.step("sync", lambda s: s.dma_start(out=onesb[:], in_=onesb_d), dma=True)
        for wi, (wd, wsb) in enumerate(((wqT_d, wq), (wkT_d, wk), (wvT_d, wv))):
            for j in range(ND):
                sl = (wi * ND + j) % 2
                p.step("sync", lambda s, wd=wd, j=j, sl=sl: s.dma_start(
                    out=wstg[:, sl * D:(sl + 1) * D],
                    in_=wd[j * 128:(j + 1) * 128, :]), dma=True)
                p.step("vector", lambda v, wsb=wsb, j=j, sl=sl: v.tensor_copy(
                    out=wsb[:, j * D:(j + 1) * D],
                    in_=wstg[:, sl * D:(sl + 1) * D]))
        for i in range(NT):
            p.step("sync", lambda s, i=i: s.dma_start(
                out=xt[:], in_=x_d[i * 128:(i + 1) * 128, :]), dma=True)
            e_ln(p, xt[:], xln[:], t0, cola)
            p.step("vector", lambda v: v.tensor_copy(out=xlnb[:], in_=xln[:]))
            e_transpose(p, xlnb, lambda j: tsc(xlnT, j), ident, ps_t[:], ND)
            def qemit(nch, ps):
                pass
            for nch in range(2):
                def mmq(t, nch=nch):
                    psx = ps_a if nch == 0 else ps_b
                    for kk in range(ND):
                        last = t.matmul(psx[:], tsc(xlnT, kk),
                                        wq[:, kk * D + nch * 512:kk * D + nch * 512 + 512],
                                        start=(kk == 0), stop=(kk == ND - 1))
                    return last
                p.step("tensor", mmq)
            e_softmax_seg(p, ps_a, ps_b, qb[:], t1, cola, seg, QS)
            for nch in range(2):
                def mmk(t, nch=nch):
                    psx = ps_a if nch == 0 else ps_b
                    for kk in range(ND):
                        last = t.matmul(psx[:], tsc(xlnT, kk),
                                        wk[:, kk * D + nch * 512:kk * D + nch * 512 + 512],
                                        start=(kk == 0), stop=(kk == ND - 1))
                    return last
                p.step("tensor", mmk)
                psx = ps_a if nch == 0 else ps_b
                p.step("scalar", lambda s, nch=nch, psx=psx: s.activation(
                    out=t0[:, nch * 512:nch * 512 + 512], in_=psx[:],
                    func=AF.Relu, scale=QS))
                p.step("scalar", lambda s, nch=nch, psx=psx: s.activation(
                    out=t1[:, nch * 512:nch * 512 + 512], in_=psx[:],
                    func=AF.Identity, scale=QS))
            p.step("vector", lambda v: v.tensor_sub(out=t2[:], in0=t1[:], in1=t0[:]))
            p.step("scalar", lambda s: s.activation(out=t2[:], in_=t2[:], func=AF.Exp))
            p.step("vector", lambda v: v.tensor_add(out=kb[:], in0=t2[:], in1=t0[:]))
            for nch in range(2):
                def mmv(t, nch=nch):
                    psx = ps_a if nch == 0 else ps_b
                    for kk in range(ND):
                        last = t.matmul(psx[:], tsc(xlnT, kk),
                                        wv[:, kk * D + nch * 512:kk * D + nch * 512 + 512],
                                        start=(kk == 0), stop=(kk == ND - 1))
                    return last
                p.step("tensor", mmv)
                psx = ps_a if nch == 0 else ps_b
                p.step("scalar", lambda s, nch=nch, psx=psx: s.activation(
                    out=vb[:, nch * 512:nch * 512 + 512], in_=psx[:],
                    func=AF.Identity))
            p.step("gpsimd", lambda g, i=i: g.dma_start(
                out=sq_d[i * 128:(i + 1) * 128, :], in_=qb[:]), dma=True)
            p.step("gpsimd", lambda g, i=i: g.dma_start(
                out=sk_d[i * 128:(i + 1) * 128, :], in_=kb[:]), dma=True)
            p.step("gpsimd", lambda g, i=i: g.dma_start(
                out=sv_d[i * 128:(i + 1) * 128, :], in_=vb[:]), dma=True)
            def mmctx(t, i=i):
                for h in range(H):
                    t.matmul(ps_ctx[:, h * 64:(h + 1) * 64],
                             kb[:, h * 64:(h + 1) * 64],
                             vb[:, h * 64:(h + 1) * 64],
                             start=True, stop=True)
                    last = t.matmul(ps_ks[:, h:h + 1],
                                    kb[:, h * 64:(h + 1) * 64],
                                    onesb[:, :],
                                    start=True, stop=True)
                return last
            p.step("tensor", mmctx)
            def vacc(v, i=i):
                if i == 0:
                    v.tensor_copy(out=kvtot[:, 0:1024], in_=ps_ctx[:])
                    return v.tensor_copy(out=kvtot[:, 1024:1040], in_=ps_ks[:])
                v.tensor_tensor(out=kvtot[:, 0:1024], in0=kvtot[:, 0:1024],
                                in1=ps_ctx[:], op=ALU.add)
                return v.tensor_tensor(out=kvtot[:, 1024:1040],
                                       in0=kvtot[:, 1024:1040],
                                       in1=ps_ks[:], op=ALU.add)
            p.step("vector", vacc)
        p.step("sync", lambda s: s.dma_start(out=kvt_d, in_=kvtot[:]), dma=True)
        return p.finish()



def e_ffnpre(p, nc, pre, x1_d, sxqT_d, sis_d, hb_d, w1T_d, names):
    pass


def build_l2():
    p = Prog()
    nc = p.nc
    x_d = nc.dram_tensor("x", [T, D], F32, kind="ExternalInput").ap()
    sq_d = nc.dram_tensor("sq", [T, D], BF16, kind="ExternalInput").ap()
    sk_d = nc.dram_tensor("sk", [T, D], BF16, kind="ExternalInput").ap()
    sv_d = nc.dram_tensor("sv", [T, D], BF16, kind="ExternalInput").ap()
    kvall_d = nc.dram_tensor("kvall", [8, 64, 1040], F32, kind="ExternalInput").ap()
    pmask_d = nc.dram_tensor("pmask", [64, 8], F32, kind="ExternalInput").ap()
    woT_d = nc.dram_tensor("woT", [D, D], F32, kind="ExternalInput").ap()
    borep_d = nc.dram_tensor("borep", [128, D], F32, kind="ExternalInput").ap()
    w1T_d = nc.dram_tensor("w1T", [D, DH4], F32, kind="ExternalInput").ap()
    ident_d = nc.dram_tensor("ident", [128, 128], BF16, kind="ExternalInput").ap()
    onesrf_d = nc.dram_tensor("onesrf", [1, 128], F32, kind="ExternalInput").ap()
    onesf_d = nc.dram_tensor("onesf", [128, 1], F32, kind="ExternalInput").ap()
    onesb_d = nc.dram_tensor("onesb", [128, 1], BF16, kind="ExternalInput").ap()
    x1_d = nc.dram_tensor("x1", [T, D], F32, kind="ExternalOutput").ap()
    sxqT_d = nc.dram_tensor("sxqT", [D, T], BF16, kind="ExternalOutput").ap()
    sis_d = nc.dram_tensor("sis", [T], F32, kind="ExternalOutput").ap()
    hb_d = nc.dram_tensor("hb", [DH4, 2], F32, kind="ExternalOutput").ap()
    with p.es:
        ident = p.sbuf("s_ident", [128, 128], BF16)
        onesrf = p.sbuf("s_onesrf", [1, 128], F32)
        onesf = p.sbuf("s_onesf", [128, 1], F32)
        onesb = p.sbuf("s_onesb", [128, 1], BF16)
        pmask = p.sbuf("s_pmask", [64, 8], F32)
        borep = p.sbuf("s_borep", [128, D], F32)
        kvsb = p.sbuf("kvsb", [64, 8 * 1040], F32)
        running = p.sbuf("running", [128, 1040], F32)
        rk = p.sbuf("rk", [128, 16], F32)
        cnbf = p.sbuf("cnbf", [128, 1024], BF16)
        wo = p.sbuf("wo", [128, ND * D], BF16)
        w1q = p.sbuf("w1q", [128, ND * DH4], BF16)
        wstg = p.sbuf("wstg", [128, DH4], F32)
        tmpw = p.sbuf("tmpw", [128, DH4], F32)
        asum = p.sbuf("asum", [128, 1], F32)
        sc1 = p.sbuf("sc1", [128, 8], F32)
        thr = p.sbuf("thr", [128, 1], F32)
        mw = p.sbuf("mw", [128, 1], F32)
        xt = p.sbuf("xt", [128, D], F32)
        qb = p.sbuf("qb", [128, D], BF16)
        kb = p.sbuf("kb", [128, D], BF16)
        vb = p.sbuf("vb", [128, D], BF16)
        qTt = p.sbuf("qTt", [128, ND * 128], BF16)
        attT = p.sbuf("attT", [128, ND * 128], BF16)
        y0 = p.sbuf("y0", [128, D], F32)
        t0 = p.sbuf("t0", [128, D], F32)
        t1 = p.sbuf("t1", [128, D], F32)
        cola = p.sbuf("cola", [128, 8], F32)
        xq = p.sbuf("xq", [128, D], BF16)
        xqTt = p.sbuf("xqTt", [128, ND * 128], BF16)
        xqcols = p.sbuf("xqcols", [128, 16], BF16)
        bnd = p.sbuf("bnd", [1, 4], F32)
        scl2 = p.sbuf("scl2", [128, 2], F32)
        hb_sb = p.sbuf("hb_sb", [128, 64], F32)
        mgc = p.sbuf("mgc", [128, 1], F32)
        ps_a = p.psum("ps_a", [128, 512], F32)
        ps_at8 = p.psum("ps_at8", [128, 1024], F32)
        ps_cx = p.psum("ps_cx", [128, 1024], F32)
        ps_ks = p.psum("ps_ks", [128, 16], F32)
        ps_t = p.psum("ps_t", [128, 128], BF16)
        ps_m = p.psum("ps_m", [128, 4], F32)
        p._ps1 = ps_m
        p._ones_row_f = onesrf
        p._mgc = mgc
        p.step("vector", lambda v: v.memset(mgc[:], MAGIC))

        for dd, ss in ((ident_d, ident), (onesrf_d, onesrf), (onesf_d, onesf),
                       (onesb_d, onesb), (pmask_d, pmask), (borep_d, borep)):
            p.step("sync", lambda s, dd=dd, ss=ss: s.dma_start(out=ss[:], in_=dd),
                   dma=True)
        p.step("sync", lambda s: s.dma_start(
            out=kvsb[:].rearrange("p (r c) -> p r c", r=8),
            in_=kvall_d.transpose([1, 0, 2])), dma=True)
        for j in range(ND):
            p.step("sync", lambda s, j=j: s.dma_start(
                out=wstg[:, 0:D],
                in_=woT_d[j * 128:(j + 1) * 128, :]), dma=True)
            p.step("vector", lambda v, j=j: v.tensor_copy(
                out=wo[:, j * D:(j + 1) * D], in_=wstg[:, 0:D]))
        def carry(v):
            v.tensor_scalar(out=running[0:64, :], in0=kvsb[:, 0:1040],
                            scalar1=pmask[:, 0:1], scalar2=None, op0=ALU.mult)
            for r in range(1, 8):
                v.drain()
                v.scalar_tensor_tensor(out=running[0:64, :],
                                       in0=kvsb[:, r * 1040:(r + 1) * 1040],
                                       scalar=pmask[:, r:r + 1],
                                       in1=running[0:64, :],
                                       op0=ALU.mult, op1=ALU.add)
            return v.drain()
        p.step("vector", carry)
        p.step("sync", lambda s: s.dma_start(out=running[64:128, :],
                                             in_=running[0:64, :]), dma=True)

        for i in range(NT):
            p.step("sync", lambda s, i=i: s.dma_start(
                out=xt[:], in_=x_d[i * 128:(i + 1) * 128, :]), dma=True)
            p.step("sync", lambda s, i=i: s.dma_start(
                out=qb[:], in_=sq_d[i * 128:(i + 1) * 128, :]), dma=True)
            p.step("sync", lambda s, i=i: s.dma_start(
                out=kb[:], in_=sk_d[i * 128:(i + 1) * 128, :]), dma=True)
            p.step("sync", lambda s, i=i: s.dma_start(
                out=vb[:], in_=sv_d[i * 128:(i + 1) * 128, :]), dma=True)
            e_transpose(p, qb, lambda j: tsc(qTt, j), ident, ps_t[:], ND)
            for b in range(2):
                def norm(v, b=b):
                    v.tensor_scalar_add(out=rk[:], in0=running[:, 1024:1040],
                                        scalar1=1e-6)
                    v.drain()
                    v.reciprocal(out=rk[:], in_=rk[:])
                    v.drain()
                    return v.tensor_tensor(
                        out=cnbf[:].rearrange("p (h e) -> p h e", e=64),
                        in0=running[:, 0:1024].rearrange("p (h e) -> p h e", e=64),
                        in1=rk[:].unsqueeze(2).broadcast_to([128, 16, 64]),
                        op=ALU.mult)
                p.step("vector", norm)
                def attmm(t, b=b):
                    for h in range(H):
                        ob = (h % 2) * 64
                        last = t.matmul(
                            ps_at8[ob:ob + 64,
                                   (h // 2) * 128 + b * 64:(h // 2) * 128 + b * 64 + 64],
                            cnbf[ob:ob + 64, h * 64:(h + 1) * 64],
                            qTt[ob:ob + 64,
                                (h // 2) * 128 + b * 64:(h // 2) * 128 + b * 64 + 64],
                            start=True, stop=True,
                            tile_position=(ob, ob))
                    return last
                p.step("tensor", attmm)
                def ctxmm(t, b=b):
                    for h in range(H):
                        for ob in (0, 64):
                            t.matmul(ps_cx[ob:ob + 64, h * 64:(h + 1) * 64],
                                     kb[b * 64:b * 64 + 64, h * 64:(h + 1) * 64],
                                     vb[b * 64:b * 64 + 64, h * 64:(h + 1) * 64],
                                     start=True, stop=True,
                                     tile_position=(b * 64, ob))
                            last = t.matmul(ps_ks[ob:ob + 64, h:h + 1],
                                            kb[b * 64:b * 64 + 64,
                                               h * 64:(h + 1) * 64],
                                            onesb[b * 64:b * 64 + 64, 0:1],
                                            start=True, stop=True,
                                            tile_position=(b * 64, ob))
                    return last
                p.step("tensor", ctxmm)
                def upd(v):
                    v.tensor_tensor(out=running[:, 0:1024], in0=running[:, 0:1024],
                                    in1=ps_cx[:], op=ALU.add)
                    v.drain()
                    return v.tensor_tensor(out=running[:, 1024:1040],
                                           in0=running[:, 1024:1040],
                                           in1=ps_ks[:], op=ALU.add)
                p.step("vector", upd)
            def evat(v):
                v.tensor_copy(out=attT[:, 0:512], in_=ps_at8[:, 0:512])
                v.drain()
                return v.tensor_copy(out=attT[:, 512:1024], in_=ps_at8[:, 512:1024])
            p.step("vector", evat)
            for nch in range(2):
                def mmo(t, nch=nch):
                    for j in range(ND):
                        last = t.matmul(ps_a[:], tsc(attT, j),
                                        wo[:, j * D + nch * 512:j * D + nch * 512 + 512],
                                        start=(j == 0), stop=(j == ND - 1))
                    return last
                p.step("tensor", mmo)
                p.step("vector", lambda v, nch=nch: v.tensor_tensor(
                    out=y0[:, nch * 512:nch * 512 + 512], in0=ps_a[:],
                    in1=borep[:, nch * 512:nch * 512 + 512], op=ALU.add))
            e_ln(p, y0[:], t1[:], t0, cola)
            p.step("vector", lambda v: v.tensor_add(out=xt[:], in0=xt[:], in1=t1[:]))
            p.step("gpsimd", lambda g, i=i: g.dma_start(
                out=x1_d[i * 128:(i + 1) * 128, :], in_=xt[:]), dma=True)
            e_ln(p, xt[:], t1[:], t0, cola)
            e_quant(p, t1[:], xq[:], t0[:], cola[:, 4:5], cola[:, 5:6], cola[:, 6:7])
            p.step("gpsimd", lambda g, i=i: g.dma_start(
                out=sis_d[i * 128:(i + 1) * 128].unsqueeze(1), in_=cola[:, 6:7]),
                dma=True)
            e_transpose(p, xq, lambda j: tsc(xqTt, j), ident, ps_t[:], ND)
            for j in range(ND):
                p.step("gpsimd", lambda g, i=i, j=j: g.dma_start(
                    out=sxqT_d[j * 128:(j + 1) * 128, i * 128:(i + 1) * 128],
                    in_=tsc(xqTt, j)), dma=True)
            if i == 0:
                def grab0(v):
                    for j in range(ND):
                        v.tensor_copy(out=xqcols[:, j * 2:j * 2 + 1],
                                      in_=xqTt[:, j * 128:j * 128 + 1])
                        v.drain()
                    return v.tensor_copy(out=bnd[0:1, 0:1], in_=cola[0:1, 6:7])
                p.step("vector", grab0)
            if i == NT - 1:
                def grabN(v):
                    for j in range(ND):
                        v.tensor_copy(out=xqcols[:, j * 2 + 1:j * 2 + 2],
                                      in_=xqTt[:, j * 128 + 127:j * 128 + 128])
                        v.drain()
                    return v.drain()
                p.step("vector", grabN)
                p.step("gpsimd", lambda g: g.dma_start(out=bnd[0:1, 1:2],
                                                       in_=cola[127:128, 6:7]),
                       dma=True)
        e_wquant(p, w1T_d, w1q, ND, DH4, wstg, asum, sc1, thr, mw, onesf, tmpw,
                 W1MEAN_N)
        def mkscl(v):
            return v.tensor_tensor(out=bnd[0:1, 2:4], in0=bnd[0:1, 0:2],
                                   in1=mw[0:1, 0:1].broadcast_to([1, 2]),
                                   op=ALU.mult)
        p.step("vector", mkscl)
        p.step("tensor", lambda t: t.matmul(ps_m[:, 0:2], onesrf[0:1, :],
                                            bnd[0:1, 2:4], start=True, stop=True))
        p.step("vector", lambda v: v.tensor_copy(out=scl2[:], in_=ps_m[:, 0:2]))
        for ci in range(NH4):
            def hbmm(t, ci=ci):
                for kk in range(ND):
                    last = t.matmul(ps_m[:, 2:4], w1q[:, kk * DH4 + ci * 128:
                                                      kk * DH4 + ci * 128 + 128],
                                    xqcols[:, kk * 2:kk * 2 + 2],
                                    start=(kk == 0), stop=(kk == ND - 1))
                return last
            p.step("tensor", hbmm)
            p.step("vector", lambda v, ci=ci: v.tensor_tensor(
                out=hb_sb[:, ci * 2:ci * 2 + 2], in0=ps_m[:, 2:4], in1=scl2[:],
                op=ALU.mult))
        p.step("sync", lambda s: s.dma_start(
            out=hb_d.rearrange("(c p) j -> p c j", p=128),
            in_=hb_sb[:].rearrange("p (c j) -> p c j", j=2)), dma=True)
        return p.finish()


def build_ffn_main(do_attn):
    p = Prog()
    nc = p.nc
    xin_d = nc.dram_tensor("xin", [T, D], F32, kind="ExternalInput").ap()
    sxqT_d = nc.dram_tensor("sxqT", [D, T], BF16, kind="ExternalInput").ap()
    sis_d = nc.dram_tensor("sis", [T], F32, kind="ExternalInput").ap()
    hball_d = nc.dram_tensor("hball", [128, 512], F32, kind="ExternalInput").ap()
    hsel_d = nc.dram_tensor("hsel", [128, 32], F32, kind="ExternalInput").ap()
    w1T_d = nc.dram_tensor("w1T", [D, DH4], F32, kind="ExternalInput").ap()
    w2T_d = nc.dram_tensor("w2T", [DH4, D], F32, kind="ExternalInput").ap()
    cw_d = nc.dram_tensor("cw", [128, NH4 * 3], F32, kind="ExternalInput").ap()
    cb_d = nc.dram_tensor("cb", [128, NH4], F32, kind="ExternalInput").ap()
    ident_d = nc.dram_tensor("ident", [128, 128], BF16, kind="ExternalInput").ap()
    identf_d = nc.dram_tensor("identf", [128, 128], F32, kind="ExternalInput").ap()
    onesrf_d = nc.dram_tensor("onesrf", [1, 128], F32, kind="ExternalInput").ap()
    onesf_d = nc.dram_tensor("onesf", [128, 1], F32, kind="ExternalInput").ap()
    xout_d = nc.dram_tensor("xout", [T, D], F32, kind="ExternalOutput").ap()
    if do_attn:
        wkT_d = nc.dram_tensor("wkT", [D, D], F32, kind="ExternalInput").ap()
        wvT_d = nc.dram_tensor("wvT", [D, D], F32, kind="ExternalInput").ap()
        wgT_d = nc.dram_tensor("wgT", [D, D], F32, kind="ExternalInput").ap()
        lq_d = nc.dram_tensor("lq", [128, 8 * 64], BF16, kind="ExternalInput").ap()
        cst_d = nc.dram_tensor("cst", [T, 64], F32, kind="ExternalInput").ap()
        maskT_d = nc.dram_tensor("maskT", [128, 128], F32, kind="ExternalInput").ap()
        sgT_d = nc.dram_tensor("sgT", [D, T], BF16, kind="ExternalOutput").ap()
        satt_d = nc.dram_tensor("satt", [D, 64], F32, kind="ExternalOutput").ap()
    with p.es:
        ident = p.sbuf("s_ident", [128, 128], BF16)
        identf = p.sbuf("s_identf", [128, 128], F32)
        onesrf = p.sbuf("s_onesrf", [1, 128], F32)
        onesf = p.sbuf("s_onesf", [128, 1], F32)
        hsel = p.sbuf("s_hsel", [128, 32], F32)
        cwsb = p.sbuf("s_cw", [128, NH4 * 3], F32)
        cbsb = p.sbuf("s_cb", [128, NH4], F32)
        w1q = p.sbuf("w1q", [128, ND * DH4], BF16)
        w2q = p.sbuf("w2q", [128, NH4 * D], BF16)
        asum = p.sbuf("asum", [128, 1], F32)
        sc1 = p.sbuf("sc1", [128, 8], F32)
        thr1 = p.sbuf("thr1", [128, 1], F32)
        mw1 = p.sbuf("mw1", [128, 1], F32)
        thr2 = p.sbuf("thr2", [128, 1], F32)
        mw2 = p.sbuf("mw2", [128, 1], F32)
        sis_sb = p.sbuf("sis_sb", [1, T], F32)
        halo = p.sbuf("halo", [128, 64], F32)
        stash = p.sbuf("stash", [128, 32], F32)
        xqb = p.sbuf("xqb", [128, ND * 257], BF16)
        srep = p.sbuf("srep", [128, 257], F32)
        hext = p.sbuf("hext", [128, 258], F32)
        conv = p.sbuf("conv", [128, 256], F32)
        hsblk = p.sbuf("hsblk", [128, NH4 * 256], F32)
        tmpw = hsblk[:, 0:DH4]
        wstg = hsblk[:, DH4:2 * DH4]
        gall = hsblk[:, DH4:DH4 + 512]
        mx = p.sbuf("mx", [128, 256], F32)
        mcol = p.sbuf("mcol", [128, 8], F32)
        s2row = p.sbuf("s2row", [1, 256], F32)
        srq = p.sbuf("srq", [128, 256], F32)
        sclc = p.sbuf("sclc", [128, 2], F32)
        tz = p.sbuf("tz", [128, 256], F32)
        hq = p.sbuf("hq", [128, 256], BF16)
        ysb = p.sbuf("ysb", [128, 2 * D], F32)
        xb = p.sbuf("xb", [128, 2 * D], F32)
        t0 = p.sbuf("t0", [128, D], F32)
        t1 = p.sbuf("t1", [128, D], F32)
        cola = p.sbuf("cola", [128, 8], F32)
        ps_h = p.psum("ps_h", [128, 257], F32)
        ps_y = [p.psum(f"ps_y{i}", [128, 512], F32) for i in range(4)]
        ps_f = p.psum("ps_f", [128, 128], F32)
        ps_s = p.psum("ps_s", [128, 257], F32)
        ps_m = ps_s
        ps_tb3 = p.psum("ps_tb3", [128, 128], BF16)
        p._ps1 = ps_s
        p._ones_row_f = onesrf

        for dd, ss in ((ident_d, ident), (identf_d, identf), (onesrf_d, onesrf),
                       (onesf_d, onesf), (hsel_d, hsel), (cw_d, cwsb),
                       (cb_d, cbsb)):
            p.step("sync", lambda s, dd=dd, ss=ss: s.dma_start(out=ss[:], in_=dd),
                   dma=True)
        p.step("sync", lambda s: s.dma_start(out=sis_sb[0:1, :],
                                             in_=sis_d.unsqueeze(0)), dma=True)
        p.step("sync", lambda s: s.dma_start(out=gall[:], in_=hball_d), dma=True)
        def mkhalo(v):
            g4 = gall[:].rearrange("p (c k) -> p c k", k=16)
            v.tensor_tensor(out=tmpw[:, 0:512].rearrange("p (c k) -> p c k", k=16),
                            in0=g4,
                            in1=hsel[:, 0:16].unsqueeze(1).broadcast_to([128, 32, 16]),
                            op=ALU.mult)
            v.drain()
            v.reduce_sum(out=halo[:].rearrange("p (c j) -> p c j", j=2)[:, :, 0],
                         in_=tmpw[:, 0:512].rearrange("p (c k) -> p c k", k=16),
                         axis=AX)
            v.drain()
            v.tensor_tensor(out=tmpw[:, 0:512].rearrange("p (c k) -> p c k", k=16),
                            in0=g4,
                            in1=hsel[:, 16:32].unsqueeze(1).broadcast_to([128, 32, 16]),
                            op=ALU.mult)
            v.drain()
            v.reduce_sum(out=halo[:].rearrange("p (c j) -> p c j", j=2)[:, :, 1],
                         in_=tmpw[:, 0:512].rearrange("p (c k) -> p c k", k=16),
                         axis=AX)
            return v.drain()
        p.step("vector", mkhalo)
        e_wquant(p, w1T_d, w1q, ND, DH4, wstg, asum, sc1, thr1, mw1, onesf, tmpw,
                 W1MEAN_N)
        e_wquant(p, w2T_d, w2q, NH4, D, wstg, asum, sc1, thr2, mw2, onesf, tmpw,
                 W1MEAN_N)

        NBLK = 8
        for b in range(NBLK):
            w = 257 if b < NBLK - 1 else 256
            for j in range(ND):
                p.step("sync", lambda s, b=b, w=w, j=j: s.dma_start(
                    out=xqb[:, j * 257:j * 257 + w],
                    in_=sxqT_d[j * 128:(j + 1) * 128, b * 256:b * 256 + w]),
                    dma=True)
            p.step("tensor", lambda t, b=b, w=w: t.matmul(
                ps_s[:, 0:w], onesrf[0:1, :], sis_sb[0:1, b * 256:b * 256 + w],
                start=True, stop=True))
            p.step("vector", lambda v, w=w: v.tensor_copy(out=srep[:, 0:w],
                                                          in_=ps_s[:, 0:w]))
            for ci in range(NH4):
                def w1mm(t, ci=ci, w=w):
                    for kk in range(ND):
                        last = t.matmul(ps_h[:, 0:w],
                                        w1q[:, kk * DH4 + ci * 128:
                                            kk * DH4 + ci * 128 + 128],
                                        xqb[:, kk * 257:kk * 257 + w],
                                        start=(kk == 0), stop=(kk == ND - 1))
                    return last
                p.step("tensor", w1mm)
                def mkhext(v, b=b, ci=ci, w=w):
                    if b == 0:
                        v.tensor_copy(out=hext[:, 0:1],
                                      in_=halo[:, ci * 2:ci * 2 + 1])
                    else:
                        v.tensor_copy(out=hext[:, 0:1], in_=stash[:, ci:ci + 1])
                    v.drain()
                    v.tensor_tensor(out=hext[:, 1:1 + w], in0=ps_h[:, 0:w],
                                    in1=srep[:, 0:w], op=ALU.mult)
                    v.drain()
                    v.tensor_single_scalar(out=hext[:, 1:1 + w],
                                           in_=hext[:, 1:1 + w],
                                           scalar=mw1[:, 0:1], op=ALU.mult)
                    v.drain()
                    if b == NBLK - 1:
                        v.tensor_copy(out=hext[:, 257:258],
                                      in_=halo[:, ci * 2 + 1:ci * 2 + 2])
                        v.drain()
                    v.tensor_copy(out=stash[:, ci:ci + 1], in_=hext[:, 256:257])
                    v.drain()
                    v.tensor_single_scalar(out=conv[:], in_=hext[:, 0:256],
                                           scalar=cwsb[:, ci * 3:ci * 3 + 1],
                                           op=ALU.mult)
                    v.drain()
                    v.scalar_tensor_tensor(out=conv[:], in0=hext[:, 1:257],
                                           scalar=cwsb[:, ci * 3 + 1:ci * 3 + 2],
                                           in1=conv[:], op0=ALU.mult, op1=ALU.add)
                    v.drain()
                    return v.scalar_tensor_tensor(out=conv[:], in0=hext[:, 2:258],
                                                  scalar=cwsb[:, ci * 3 + 2:ci * 3 + 3],
                                                  in1=conv[:], op0=ALU.mult,
                                                  op1=ALU.add)
                p.step("vector", mkhext)
                p.step("scalar", lambda s, ci=ci: s.activation(
                    out=hsblk[:, ci * 256:ci * 256 + 256], in_=conv[:],
                    func=AF.Silu, bias=cbsb[:, ci:ci + 1]))
                p.step("scalar", lambda s, ci=ci: s.activation(
                    out=conv[:], in_=hsblk[:, ci * 256:ci * 256 + 256],
                    func=AF.Abs))
                if ci == 0:
                    p.step("vector", lambda v: v.tensor_copy(out=mx[:],
                                                             in_=conv[:]))
                else:
                    p.step("vector", lambda v: v.tensor_tensor(
                        out=mx[:], in0=mx[:], in1=conv[:], op=ALU.max))
            for ts in range(2):
                p.step("tensor", lambda t, ts=ts: t.transpose(
                    ps_f[:], mx[:, ts * 128:ts * 128 + 128], identf[:]))
                def mk_s(v, ts=ts):
                    v.reduce_max(out=mcol[:, ts:ts + 1], in_=ps_f[:], axis=AX)
                    v.drain()
                    v.tensor_scalar_max(out=mcol[:, ts:ts + 1],
                                        in0=mcol[:, ts:ts + 1], scalar1=1e-5)
                    v.drain()
                    v.reciprocal(out=mcol[:, 2 + ts:3 + ts], in_=mcol[:, ts:ts + 1])
                    v.drain()
                    v.tensor_scalar_mul(out=mcol[:, 2 + ts:3 + ts],
                                        in0=mcol[:, 2 + ts:3 + ts], scalar1=127.0)
                    v.drain()
                    v.tensor_scalar_mul(out=mcol[:, 4 + ts:5 + ts],
                                        in0=mcol[:, ts:ts + 1], scalar1=1.0 / 127.0)
                    v.drain()
                    return v.tensor_single_scalar(out=mcol[:, 6 + ts:7 + ts],
                                                  in_=mcol[:, 4 + ts:5 + ts],
                                                  scalar=mw2[:, 0:1], op=ALU.mult)
                p.step("vector", mk_s)
            for ts in range(2):
                p.step("tensor", lambda t, ts=ts: t.transpose(
                    ps_f[0:1, :], mcol[:, 2 + ts:3 + ts], identf[:]))
                p.step("vector", lambda v, ts=ts: v.tensor_copy(
                    out=s2row[0:1, ts * 128:ts * 128 + 128], in_=ps_f[0:1, :]))
            for ts in range(2):
                p.step("tensor", lambda t, ts=ts: t.matmul(
                    ps_s[:, ts * 128:ts * 128 + 128], onesrf[0:1, :],
                    s2row[0:1, ts * 128:ts * 128 + 128], start=True, stop=True))
            p.step("vector", lambda v: v.tensor_copy(out=srq[:],
                                                     in_=ps_s[:, 0:256]))
            for ci in range(NH4):
                def qz(v, ci=ci):
                    v.tensor_tensor(out=tz[:], in0=hsblk[:, ci * 256:ci * 256 + 256],
                                    in1=srq[:], op=ALU.mult)
                    v.drain()
                    v.tensor_scalar_add(out=tz[:], in0=tz[:], scalar1=MAGIC)
                    v.drain()
                    v.tensor_copy(out=tz[:].bitcast(I32), in_=tz[:].bitcast(I32))
                    v.drain()
                    return v.tensor_scalar_sub(out=hq[:], in0=tz[:], scalar1=MAGIC)
                p.step("vector", qz)
                def w2mm(t, ci=ci):
                    for ts in range(2):
                        for dc in range(2):
                            last = t.matmul(ps_y[ts * 2 + dc][:],
                                            hq[:, ts * 128:ts * 128 + 128],
                                            w2q[:, ci * D + dc * 512:
                                                ci * D + dc * 512 + 512],
                                            start=(ci == 0), stop=(ci == NH4 - 1))
                    return last
                p.step("tensor", w2mm)
            p.step("sync", lambda s, b=b: s.dma_start(
                out=xb[:].rearrange("p (s d) -> p s d", s=2),
                in_=xin_d[b * 256:b * 256 + 256, :].rearrange(
                    "(s p) d -> p s d", p=128)), dma=True)
            for ts in range(2):
                for dc in range(2):
                    p.step("scalar", lambda s, ts=ts, dc=dc: s.activation(
                        out=ysb[:, ts * D + dc * 512:ts * D + dc * 512 + 512],
                        in_=ps_y[ts * 2 + dc][:], func=AF.Identity,
                        scale=mcol[:, 6 + ts:7 + ts]))
            for ts in range(2):
                e_ln(p, ysb[:, ts * D:(ts + 1) * D], t1[:], t0, cola)
                p.step("vector", lambda v, ts=ts: v.tensor_tensor(
                    out=xb[:, ts * D:(ts + 1) * D], in0=xb[:, ts * D:(ts + 1) * D],
                    in1=t1[:], op=ALU.add))
            p.step("gpsimd", lambda g, b=b: g.dma_start(
                out=xout_d[b * 256:b * 256 + 256, :].rearrange(
                    "(s p) d -> p s d", p=128),
                in_=xb[:].rearrange("p (s d) -> p s d", s=2)), dma=True)

        if do_attn:
            wk = w1q[:, 0:ND * D]
            wv = w1q[:, ND * D:2 * ND * D]
            wg = w1q[:, 2 * ND * D:3 * ND * D]
            xlnT = w2q[:, 0:1024]
            kTt = w2q[:, 1024:2048]
            gTt = w2q[:, 2048:3072]
            lq = w2q[:, 3072:3584]
            vb2 = w2q[:, 3584:4608]
            kbf = w2q[:, 4608:5632]
            xlnb = w2q[:, 5632:6656]
            pbf = w2q[:, 6656:6784]
            pTb = w2q[:, 6784:6912]
            gb = w2q[:, 6912:7936]
            kf = hsblk[:, 0:1024]
            krot = hsblk[:, 1024:2048]
            scsb = hsblk[:, 2048:2176]
            att = hsblk[:, 2176:2688]
            cs_sb = hsblk[:, 2688:2752]
            maskT = hsblk[:, 2752:2880]
            xt = hsblk[:, 2880:3904]
            seg = p.sbuf("seg", [128, 16], F32)
            ps_a = ps_y[0]
            ps_b = ps_y[1]
            ps_sc = ps_y[2]
            ps_at = ps_y[3]
            ps_tb = ps_tb3[:]
            for dd, ss in ((lq_d, lq), (maskT_d, maskT)):
                p.step("sync", lambda s, dd=dd, ss=ss: s.dma_start(
                    out=ss[:], in_=dd), dma=True)
            for wd, wsb in ((wkT_d, wk), (wvT_d, wv), (wgT_d, wg)):
                for j in range(ND):
                    p.step("sync", lambda s, wd=wd, j=j: s.dma_start(
                        out=wstg[:, 0:D],
                        in_=wd[j * 128:(j + 1) * 128, :]), dma=True)
                    p.step("vector", lambda v, wsb=wsb, j=j: v.tensor_copy(
                        out=wsb[:, j * D:(j + 1) * D],
                        in_=wstg[:, 0:D]))
            for i in range(NT):
                p.step("sync", lambda s, i=i: s.dma_start(
                    out=xt[:], in_=xout_d[i * 128:(i + 1) * 128, :]), dma=True)
                p.step("sync", lambda s, i=i: s.dma_start(
                    out=cs_sb[:], in_=cst_d[i * 128:(i + 1) * 128, :]), dma=True)
                e_ln(p, xt[:], t1[:], t0, cola)
                p.step("vector", lambda v: v.tensor_copy(out=xlnb[:], in_=t1[:]))
                e_transpose(p, xlnb, lambda j: tsc(xlnT, j), ident, ps_tb, ND)
                for nch in range(2):
                    def mmk(t, nch=nch):
                        psx = ps_a if nch == 0 else ps_b
                        for kk in range(ND):
                            last = t.matmul(psx[:], tsc(xlnT, kk),
                                            wk[:, kk * D + nch * 512:
                                               kk * D + nch * 512 + 512],
                                            start=(kk == 0), stop=(kk == ND - 1))
                        return last
                    p.step("tensor", mmk)
                    psx = ps_a if nch == 0 else ps_b
                    p.step("scalar", lambda s, nch=nch, psx=psx: s.activation(
                        out=kf[:, nch * 512:nch * 512 + 512], in_=psx[:],
                        func=AF.Identity))
                def rope(v):
                    k4 = kf[:].rearrange("p (h d two) -> p two h d", h=16, d=32)
                    o4 = krot[:].rearrange("p (h d two) -> p two h d", h=16, d=32)
                    cos3 = cs_sb[:, 0:32].unsqueeze(1).broadcast_to([128, 16, 32])
                    sin3 = cs_sb[:, 32:64].unsqueeze(1).broadcast_to([128, 16, 32])
                    v.tensor_tensor(out=o4[:, 0], in0=k4[:, 0], in1=cos3,
                                    op=ALU.mult)
                    v.drain()
                    v.tensor_tensor(out=t0[:, 0:512].rearrange(
                        "p (h d) -> p h d", d=32), in0=k4[:, 1], in1=sin3,
                        op=ALU.mult)
                    v.drain()
                    v.tensor_tensor(out=o4[:, 0], in0=o4[:, 0],
                                    in1=t0[:, 0:512].rearrange(
                                        "p (h d) -> p h d", d=32),
                                    op=ALU.subtract)
                    v.drain()
                    v.tensor_tensor(out=o4[:, 1], in0=k4[:, 0], in1=sin3,
                                    op=ALU.mult)
                    v.drain()
                    v.tensor_tensor(out=t0[:, 0:512].rearrange(
                        "p (h d) -> p h d", d=32), in0=k4[:, 1], in1=cos3,
                        op=ALU.mult)
                    v.drain()
                    v.tensor_tensor(out=o4[:, 1], in0=o4[:, 1],
                                    in1=t0[:, 0:512].rearrange(
                                        "p (h d) -> p h d", d=32),
                                    op=ALU.add)
                    v.drain()
                    return v.tensor_copy(out=kbf[:], in_=krot[:])
                p.step("vector", rope)
                e_transpose(p, kbf, lambda j: tsc(kTt, j), ident, ps_tb, ND)
                for nch in range(2):
                    def mmv(t, nch=nch):
                        psx = ps_a if nch == 0 else ps_b
                        for kk in range(ND):
                            last = t.matmul(psx[:], tsc(xlnT, kk),
                                            wv[:, kk * D + nch * 512:
                                               kk * D + nch * 512 + 512],
                                            start=(kk == 0), stop=(kk == ND - 1))
                        return last
                    p.step("tensor", mmv)
                    psx = ps_a if nch == 0 else ps_b
                    p.step("scalar", lambda s, nch=nch, psx=psx: s.activation(
                        out=vb2[:, nch * 512:nch * 512 + 512], in_=psx[:],
                        func=AF.Identity))
                for nch in range(2):
                    def mmg(t, nch=nch):
                        psx = ps_a if nch == 0 else ps_b
                        for kk in range(ND):
                            last = t.matmul(psx[:], tsc(xlnT, kk),
                                            wg[:, kk * D + nch * 512:
                                               kk * D + nch * 512 + 512],
                                            start=(kk == 0), stop=(kk == ND - 1))
                        return last
                    p.step("tensor", mmg)
                e_softmax_seg(p, ps_a, ps_b, gb[:], t1, cola, seg, L1SC)
                e_transpose(p, gb, lambda j: tsc(gTt, j), ident, ps_tb, ND)
                for j in range(ND):
                    p.step("gpsimd", lambda g, i=i, j=j: g.dma_start(
                        out=sgT_d[j * 128:(j + 1) * 128, i * 128:(i + 1) * 128],
                        in_=tsc(gTt, j)), dma=True)
                for hp in range(8):
                    def scmm(t, hp=hp):
                        for hs in range(2):
                            last = t.matmul(
                                ps_sc[hs * 64:hs * 64 + 64, 0:128],
                                lq[hs * 64:hs * 64 + 64, hp * 64:hp * 64 + 64],
                                kTt[hs * 64:hs * 64 + 64,
                                    hp * 128:hp * 128 + 128],
                                start=True, stop=True,
                                tile_position=(hs * 64, hs * 64))
                        return last
                    p.step("tensor", scmm)
                    p.step("scalar", lambda s: s.activation(
                        out=scsb[:], in_=ps_sc[:, 0:128], func=AF.Identity,
                        scale=L1SC))
                    def smax(v):
                        v.tensor_tensor(out=scsb[:], in0=scsb[:], in1=maskT[:],
                                        op=ALU.add)
                        v.drain()
                        return v.reduce_max(out=cola[:, 0:1], in_=scsb[:],
                                            axis=AX, negate=True)
                    p.step("vector", smax)
                    p.step("scalar", lambda s: s.activation(
                        out=scsb[:], in_=scsb[:], func=AF.Exp, bias=cola[:, 0:1]))
                    def sdiv(v):
                        v.reduce_sum(out=cola[:, 1:2], in_=scsb[:], axis=AX)
                        v.drain()
                        v.reciprocal(out=cola[:, 1:2], in_=cola[:, 1:2])
                        v.drain()
                        return v.tensor_single_scalar(out=pbf[:], in_=scsb[:],
                                                      scalar=cola[:, 1:2],
                                                      op=ALU.mult)
                    p.step("vector", sdiv)
                    p.step("tensor", lambda t: t.transpose(ps_tb, pbf[:],
                                                           ident[:]))
                    p.step("vector", lambda v: v.tensor_copy(out=pTb[:],
                                                             in_=ps_tb))
                    def atmm(t, hp=hp):
                        for hs in range(2):
                            last = t.matmul(
                                ps_at[hs * 64:hs * 64 + 64, 0:64],
                                vb2[:, (2 * hp + hs) * 64:(2 * hp + hs) * 64 + 64],
                                pTb[:, hs * 64:hs * 64 + 64],
                                start=True, stop=True,
                                tile_position=(0, hs * 64))
                        return last
                    p.step("tensor", atmm)
                    def atacc(v, i=i, hp=hp):
                        if i == 0:
                            return v.tensor_copy(out=att[:, hp * 64:hp * 64 + 64],
                                                 in_=ps_at[:, 0:64])
                        return v.tensor_tensor(out=att[:, hp * 64:hp * 64 + 64],
                                               in0=att[:, hp * 64:hp * 64 + 64],
                                               in1=ps_at[:, 0:64], op=ALU.add)
                    p.step("vector", atacc)
            p.step("sync", lambda s: s.dma_start(
                out=satt_d.rearrange("(j p) l -> p j l", p=128),
                in_=att[:].rearrange("p (j l) -> p j l", l=64)), dma=True)
        return p.finish()


def build_l4():
    p = Prog()
    nc = p.nc
    x2_d = nc.dram_tensor("x2", [T, D], F32, kind="ExternalInput").ap()
    sattall_d = nc.dram_tensor("sattall", [8, D, 64], F32, kind="ExternalInput").ap()
    sgT_d = nc.dram_tensor("sgT", [D, T], BF16, kind="ExternalInput").ap()
    wpT_d = nc.dram_tensor("wpT", [D, D], F32, kind="ExternalInput").ap()
    w1T_d = nc.dram_tensor("w1T", [D, DH4], F32, kind="ExternalInput").ap()
    ident_d = nc.dram_tensor("ident", [128, 128], BF16, kind="ExternalInput").ap()
    onesrf_d = nc.dram_tensor("onesrf", [1, 128], F32, kind="ExternalInput").ap()
    onesf_d = nc.dram_tensor("onesf", [128, 1], F32, kind="ExternalInput").ap()
    x3_d = nc.dram_tensor("x3", [T, D], F32, kind="ExternalOutput").ap()
    sxqT_d = nc.dram_tensor("sxqT", [D, T], BF16, kind="ExternalOutput").ap()
    sis_d = nc.dram_tensor("sis", [T], F32, kind="ExternalOutput").ap()
    hb_d = nc.dram_tensor("hb", [DH4, 2], F32, kind="ExternalOutput").ap()
    with p.es:
        ident = p.sbuf("s_ident", [128, 128], BF16)
        onesrf = p.sbuf("s_onesrf", [1, 128], F32)
        onesf = p.sbuf("s_onesf", [128, 1], F32)
        att8 = p.sbuf("att8", [128, 8 * 512], F32)
        atsum = p.sbuf("atsum", [128, 512], F32)
        atbf = p.sbuf("atbf", [128, 512], BF16)
        aT = p.sbuf("aT", [128, 16 * 64], BF16)
        wp = p.sbuf("wp", [128, ND * D], BF16)
        w1q = p.sbuf("w1q", [128, ND * DH4], BF16)
        wstg = p.sbuf("wstg", [128, DH4], F32)
        tmpw = p.sbuf("tmpw", [128, DH4], F32)
        asum = p.sbuf("asum", [128, 1], F32)
        sc1 = p.sbuf("sc1", [128, 8], F32)
        thr = p.sbuf("thr", [128, 1], F32)
        mw = p.sbuf("mw", [128, 1], F32)
        gTt = p.sbuf("gTt", [128, ND * 128], BF16)
        wTt = p.sbuf("wTt", [128, ND * 128], BF16)
        xt = p.sbuf("xt", [128, D], F32)
        y0 = p.sbuf("y0", [128, D], F32)
        t0 = p.sbuf("t0", [128, D], F32)
        t1 = p.sbuf("t1", [128, D], F32)
        cola = p.sbuf("cola", [128, 8], F32)
        xq = p.sbuf("xq", [128, D], BF16)
        xqTt = p.sbuf("xqTt", [128, ND * 128], BF16)
        xqcols = p.sbuf("xqcols", [128, 16], BF16)
        bnd = p.sbuf("bnd", [1, 4], F32)
        scl2 = p.sbuf("scl2", [128, 2], F32)
        hb_sb = p.sbuf("hb_sb", [128, 64], F32)
        mgc = p.sbuf("mgc", [128, 1], F32)
        ps_a = p.psum("ps_a", [128, 512], F32)
        ps_w = p.psum("ps_w", [128, 128], F32)
        ps_t = p.psum("ps_t", [128, 128], BF16)
        ps_tb2 = p.psum("ps_tb2", [64, 64], BF16)
        ps_m = p.psum("ps_m", [128, 4], F32)
        p._ps1 = ps_m
        p._ones_row_f = onesrf
        p._mgc = mgc
        p.step("vector", lambda v: v.memset(mgc[:], MAGIC))

        for dd, ss in ((ident_d, ident), (onesrf_d, onesrf), (onesf_d, onesf)):
            p.step("sync", lambda s, dd=dd, ss=ss: s.dma_start(out=ss[:], in_=dd),
                   dma=True)
        for r in range(8):
            p.step("sync", lambda s, r=r: s.dma_start(
                out=att8[:, r * 512:(r + 1) * 512].rearrange(
                    "p (j l) -> p j l", l=64),
                in_=sattall_d[r].rearrange("(j p) l -> p j l", p=128)), dma=True)
        def asum_f(v):
            v.tensor_copy(out=atsum[:], in_=att8[:, 0:512])
            for r in range(1, 8):
                v.drain()
                v.tensor_tensor(out=atsum[:], in0=atsum[:],
                                in1=att8[:, r * 512:(r + 1) * 512], op=ALU.add)
            v.drain()
            return v.tensor_copy(out=atbf[:], in_=atsum[:])
        p.step("vector", asum_f)
        for h in range(H):
            j, off = h // 2, (h % 2) * 64
            p.step("tensor", lambda t, j=j, off=off: t.transpose(
                ps_tb2[:], atbf[off:off + 64, j * 64:j * 64 + 64],
                ident[off:off + 64, off:off + 64]))
            p.step("vector", lambda v, h=h: v.tensor_copy(
                out=aT[0:64, h * 64:h * 64 + 64], in_=ps_tb2[:]))
        p.step("sync", lambda s: s.dma_start(out=aT[64:128, :],
                                             in_=aT[0:64, :]), dma=True)
        for j in range(ND):
            p.step("sync", lambda s, j=j: s.dma_start(
                out=wstg[:, 0:D], in_=wpT_d[j * 128:(j + 1) * 128, :]), dma=True)
            p.step("vector", lambda v, j=j: v.tensor_copy(
                out=wp[:, j * D:(j + 1) * D], in_=wstg[:, 0:D]))
        for i in range(NT):
            for j in range(ND):
                p.step("sync", lambda s, i=i, j=j: s.dma_start(
                    out=tsc(gTt, j),
                    in_=sgT_d[j * 128:(j + 1) * 128, i * 128:(i + 1) * 128]),
                    dma=True)
            p.step("sync", lambda s, i=i: s.dma_start(
                out=xt[:], in_=x2_d[i * 128:(i + 1) * 128, :]), dma=True)
            for hp in range(8):
                def wmm(t, hp=hp):
                    for hs in range(2):
                        h = 2 * hp + hs
                        ob = hs * 64
                        last = t.matmul(ps_w[ob:ob + 64, :],
                                        aT[ob:ob + 64, h * 64:h * 64 + 64],
                                        gTt[ob:ob + 64,
                                            hp * 128:hp * 128 + 128],
                                        start=True, stop=True,
                                        tile_position=(ob, ob))
                    return last
                p.step("tensor", wmm)
                p.step("vector", lambda v, hp=hp: v.tensor_copy(
                    out=tsc(wTt, hp), in_=ps_w[:]))
            for nch in range(2):
                def mmp(t, nch=nch):
                    for j in range(ND):
                        last = t.matmul(ps_a[:], tsc(wTt, j),
                                        wp[:, j * D + nch * 512:j * D + nch * 512 + 512],
                                        start=(j == 0), stop=(j == ND - 1))
                    return last
                p.step("tensor", mmp)
                p.step("scalar", lambda s, nch=nch: s.activation(
                    out=y0[:, nch * 512:nch * 512 + 512], in_=ps_a[:],
                    func=AF.Identity))
            e_ln(p, y0[:], t1[:], t0, cola)
            p.step("vector", lambda v: v.tensor_add(out=xt[:], in0=xt[:], in1=t1[:]))
            p.step("gpsimd", lambda g, i=i: g.dma_start(
                out=x3_d[i * 128:(i + 1) * 128, :], in_=xt[:]), dma=True)
            e_ln(p, xt[:], t1[:], t0, cola)
            e_quant(p, t1[:], xq[:], t0[:], cola[:, 4:5], cola[:, 5:6], cola[:, 6:7])
            p.step("gpsimd", lambda g, i=i: g.dma_start(
                out=sis_d[i * 128:(i + 1) * 128].unsqueeze(1), in_=cola[:, 6:7]),
                dma=True)
            e_transpose(p, xq, lambda j: tsc(xqTt, j), ident, ps_t[:], ND)
            for j in range(ND):
                p.step("gpsimd", lambda g, i=i, j=j: g.dma_start(
                    out=sxqT_d[j * 128:(j + 1) * 128, i * 128:(i + 1) * 128],
                    in_=tsc(xqTt, j)), dma=True)
            if i == 0:
                def grab0(v):
                    for j in range(ND):
                        v.tensor_copy(out=xqcols[:, j * 2:j * 2 + 1],
                                      in_=xqTt[:, j * 128:j * 128 + 1])
                        v.drain()
                    return v.tensor_copy(out=bnd[0:1, 0:1], in_=cola[0:1, 6:7])
                p.step("vector", grab0)
            if i == NT - 1:
                def grabN(v):
                    for j in range(ND):
                        v.tensor_copy(out=xqcols[:, j * 2 + 1:j * 2 + 2],
                                      in_=xqTt[:, j * 128 + 127:j * 128 + 128])
                        v.drain()
                    return v.drain()
                p.step("vector", grabN)
                p.step("gpsimd", lambda g: g.dma_start(out=bnd[0:1, 1:2],
                                                       in_=cola[127:128, 6:7]),
                       dma=True)
        e_wquant(p, w1T_d, w1q, ND, DH4, wstg, asum, sc1, thr, mw, onesf, tmpw,
                 W1MEAN_N)
        def mkscl(v):
            return v.tensor_tensor(out=bnd[0:1, 2:4], in0=bnd[0:1, 0:2],
                                   in1=mw[0:1, 0:1].broadcast_to([1, 2]),
                                   op=ALU.mult)
        p.step("vector", mkscl)
        p.step("tensor", lambda t: t.matmul(ps_m[:, 0:2], onesrf[0:1, :],
                                            bnd[0:1, 2:4], start=True, stop=True))
        p.step("vector", lambda v: v.tensor_copy(out=scl2[:], in_=ps_m[:, 0:2]))
        for ci in range(NH4):
            def hbmm(t, ci=ci):
                for kk in range(ND):
                    last = t.matmul(ps_m[:, 2:4], w1q[:, kk * DH4 + ci * 128:
                                                      kk * DH4 + ci * 128 + 128],
                                    xqcols[:, kk * 2:kk * 2 + 2],
                                    start=(kk == 0), stop=(kk == ND - 1))
                return last
            p.step("tensor", hbmm)
            p.step("vector", lambda v, ci=ci: v.tensor_tensor(
                out=hb_sb[:, ci * 2:ci * 2 + 2], in0=ps_m[:, 2:4], in1=scl2[:],
                op=ALU.mult))
        p.step("sync", lambda s: s.dma_start(
            out=hb_d.rearrange("(c p) j -> p c j", p=128),
            in_=hb_sb[:].rearrange("p (c j) -> p c j", j=2)), dma=True)
        return p.finish()



def _host_consts():
    f32 = np.float32
    ident_bf = np.eye(128, dtype=f32).astype(NPBF)
    identf = np.eye(128, dtype=f32)
    onesb = np.ones((128, 1), f32).astype(NPBF)
    onesf = np.ones((128, 1), f32)
    onesrf = np.ones((1, 128), f32)
    return ident_bf, identf, onesb, onesf, onesrf


def kernel(**inputs):
    f32 = np.float32
    xfull = np.ascontiguousarray(
        np.asarray(inputs["L0_attn_x"], f32).reshape(16384, D))
    shards = [np.ascontiguousarray(xfull[c * T:(c + 1) * T]) for c in range(NCORE)]
    tr = lambda w: np.ascontiguousarray(np.asarray(w, f32).T)
    ident_bf, identf, onesb, onesf, onesrf = _host_consts()
    cores = list(range(NCORE))

    nc1 = build_l1()
    im1 = [{"x": shards[c], "wqT": tr(inputs["L0_attn_Wq"]),
            "wkT": tr(inputs["L0_attn_Wk"]), "wvT": tr(inputs["L0_attn_Wv"]),
            "ident": ident_bf, "onesb": onesb} for c in cores]
    r1 = run_bass_kernel_spmd(nc1, im1, core_ids=cores).results

    kvall = np.stack([np.asarray(r1[c]["kvtot"], f32) for c in cores])
    pmasks = [np.broadcast_to(
        (np.arange(8, dtype=f32) < c).astype(f32)[None, :], (64, 8)).copy()
        for c in cores]
    borep = np.broadcast_to(np.asarray(inputs["L0_attn_bo"], f32)[None, :],
                            (128, D)).copy()

    nc2 = build_l2()
    im2 = [{"x": shards[c], "sq": np.asarray(r1[c]["sq"]),
            "sk": np.asarray(r1[c]["sk"]), "sv": np.asarray(r1[c]["sv"]),
            "kvall": kvall, "pmask": pmasks[c], "woT": tr(inputs["L0_attn_Wo"]),
            "borep": borep, "w1T": tr(inputs["L0_ffn_w1"]), "ident": ident_bf,
            "onesrf": onesrf, "onesf": onesf, "onesb": onesb} for c in cores]
    r2 = run_bass_kernel_spmd(nc2, im2, core_ids=cores).results

    hball = np.stack([np.asarray(r2[c]["hb"], f32) for c in cores])
    hball = np.ascontiguousarray(hball.reshape(8, 32, 128, 2)
                                 .transpose(2, 1, 0, 3).reshape(128, 512))
    def hsel_core(c):
        sel = np.zeros((128, 32), f32)
        if c > 0:
            sel[:, (c - 1) * 2 + 1] = 1.0
        if c < 7:
            sel[:, 16 + (c + 1) * 2] = 1.0
        return sel
    cw = np.asarray(inputs["L0_ffn_conv_w"], f32)[:, 0, :]
    cw_sb = np.ascontiguousarray(cw.reshape(NH4, 128, 3).transpose(1, 0, 2)
                                 .reshape(128, NH4 * 3))
    cb = np.asarray(inputs["L0_ffn_conv_b"], f32)
    cb_sb = np.ascontiguousarray(cb.reshape(NH4, 128).T)
    dd = np.arange(32, dtype=np.float64)
    inv = 1.0 / (10000.0 ** (2 * dd / 64.0))
    lqs = np.asarray(inputs["L1_attn_latent_queries"], f32)[0]
    lq_sb = np.zeros((128, 8 * 64), f32)
    for h in range(16):
        blk_ = lqs[:, h, :].T
        lq_sb[(h % 2) * 64:(h % 2) * 64 + 64, (h // 2) * 64:(h // 2) * 64 + 64] = blk_
    maskT = np.where(np.arange(128)[None, :] <= np.arange(64)[:, None],
                     0.0, -1e30).astype(f32)
    maskT2 = np.concatenate([maskT, maskT], axis=0)

    nc3 = build_ffn_main(True)
    im3 = []
    for c in cores:
        tpos = np.arange(c * T, (c + 1) * T, dtype=np.float64)
        ang = tpos[:, None] * inv[None, :]
        cst = np.concatenate([np.cos(ang), np.sin(ang)], 1).astype(f32)
        im3.append({"xin": np.asarray(r2[c]["x1"], f32),
                    "sxqT": np.asarray(r2[c]["sxqT"]),
                    "sis": np.asarray(r2[c]["sis"], f32), "hball": hball,
                    "hsel": hsel_core(c), "w1T": tr(inputs["L0_ffn_w1"]),
                    "w2T": tr(inputs["L0_ffn_w2"]), "cw": cw_sb, "cb": cb_sb,
                    "ident": ident_bf, "identf": identf, "onesrf": onesrf,
                    "onesf": onesf, "wkT": tr(inputs["L1_attn_W_k"]),
                    "wvT": tr(inputs["L1_attn_W_v"]),
                    "wgT": tr(inputs["L1_attn_W_g"]),
                    "lq": lq_sb.astype(NPBF), "cst": cst, "maskT": maskT2})
    r3 = run_bass_kernel_spmd(nc3, im3, core_ids=cores).results

    sattall = np.stack([np.asarray(r3[c]["satt"], f32) for c in cores])
    nc4 = build_l4()
    im4 = [{"x2": np.asarray(r3[c]["xout"], f32), "sattall": sattall,
            "sgT": np.asarray(r3[c]["sgT"]), "wpT": tr(inputs["L1_attn_W_p"]),
            "w1T": tr(inputs["L1_ffn_w1"]), "ident": ident_bf,
            "onesrf": onesrf, "onesf": onesf} for c in cores]
    r4 = run_bass_kernel_spmd(nc4, im4, core_ids=cores).results

    hball1 = np.stack([np.asarray(r4[c]["hb"], f32) for c in cores])
    hball1 = np.ascontiguousarray(hball1.reshape(8, 32, 128, 2)
                                  .transpose(2, 1, 0, 3).reshape(128, 512))
    cw1 = np.asarray(inputs["L1_ffn_conv_w"], f32)[:, 0, :]
    cw1_sb = np.ascontiguousarray(cw1.reshape(NH4, 128, 3).transpose(1, 0, 2)
                                  .reshape(128, NH4 * 3))
    cb1 = np.asarray(inputs["L1_ffn_conv_b"], f32)
    cb1_sb = np.ascontiguousarray(cb1.reshape(NH4, 128).T)
    nc5 = build_ffn_main(False)
    im5 = [{"xin": np.asarray(r4[c]["x3"], f32),
            "sxqT": np.asarray(r4[c]["sxqT"]),
            "sis": np.asarray(r4[c]["sis"], f32), "hball": hball1,
            "hsel": hsel_core(c), "w1T": tr(inputs["L1_ffn_w1"]),
            "w2T": tr(inputs["L1_ffn_w2"]), "cw": cw1_sb, "cb": cb1_sb,
            "ident": ident_bf, "identf": identf, "onesrf": onesrf,
            "onesf": onesf} for c in cores]
    r5 = run_bass_kernel_spmd(nc5, im5, core_ids=cores).results

    out = np.concatenate([np.asarray(r5[c]["xout"], f32) for c in cores], 0)
    return out.reshape(1, 16384, D)


# version 15
# speedup vs baseline: 1.1070x; 1.1070x over previous
import numpy as np
import ml_dtypes
import concourse.bass as bass
import concourse.mybir as mybir
from concourse.bass_utils import run_bass_kernel_spmd
from contextlib import ExitStack

F32 = mybir.dt.float32
BF16 = mybir.dt.bfloat16
I32 = mybir.dt.int32
AF = mybir.ActivationFunctionType
ALU = mybir.AluOpType
AX = mybir.AxisListType.X
NPBF = ml_dtypes.bfloat16

NCORE = 8
T = 2048
NT = 16
D = 1024
ND = 8
H = 16
NBUK = 32
DH4 = 4096
NH4 = 32
MAGIC = 12582912.0
QS = 64 ** -0.25
L1SC = 0.125
W1MEAN_N = 4096 * 1024


class Prog:
    def __init__(self, name="prog"):
        self.nc = bass.Bass("TRN2", target_bir_lowering=False)
        self.es = ExitStack()
        self.steps = []
        self._ctx_open = False

    def __enter__(self):
        self.es.__enter__()
        return self

    def sbuf(self, name, shape, dtype):
        return self.es.enter_context(self.nc.sbuf_tensor(name, shape, dtype))

    def psum(self, name, shape, dtype):
        return self.es.enter_context(self.nc.psum_tensor(name, shape, dtype))

    def step(self, eng, fn, dma=False, n_dma=1, relax=0):
        self.steps.append((eng, fn, 16 * n_dma if dma else 1, relax))

    def finish(self):
        SEG = 3600
        ENGS = ("sync", "vector", "scalar", "tensor", "gpsimd")
        steps = self.steps
        n = len(steps)
        cumT = [0] * (n + 1)
        expectG = [0] * (n + 1)
        for si, st in enumerate(steps):
            t_inc = st[2] if st[0] == "tensor" else 0
            g_inc = st[2] if st[0] != "tensor" else 0
            cumT[si + 1] = cumT[si] + t_inc
            expectG[si + 1] = expectG[si] + g_inc
        assert cumT[-1] < 4000, cumT[-1]
        segs = expectG[-1] // SEG + 1
        Gs = [self.es.enter_context(self.nc.semaphore(f"G{k}"))
              for k in range(segs)]
        Tsem = self.es.enter_context(self.nc.semaphore("Tm"))
        segof = [expectG[si] // SEG for si in range(n)]
        locG = [expectG[si] - segof[si] * SEG for si in range(n)]
        runseg = [0] * segs
        cumseg = []
        for si, st in enumerate(steps):
            if st[0] != "tensor":
                runseg[segof[si]] += st[2]
            cumseg.append(tuple(runseg))
        seg_tot = tuple(runseg)

        def g_wait(eng, hw, through):
            need = cumseg[through]
            for k in range(segs):
                if need[k] > hw[k]:
                    eng.wait_ge(Gs[k], need[k])
                    hw[k] = need[k]
        blk = self.es.enter_context(self.nc.Block())

        class DrainProxy:
            def __init__(self, eng):
                object.__setattr__(self, "_e", eng)
                object.__setattr__(self, "_n", 0)

            def __getattr__(self, name):
                f = getattr(self._e, name)
                if not callable(f):
                    return f
                if name == "drain":
                    def wrapd(*a, **k):
                        object.__setattr__(self, "_n", 0)
                        return f(*a, **k)
                    return wrapd
                if name == "wait_ge":
                    return f

                def wrap(*a, **k):
                    if self._n > 0:
                        self._e.drain()
                    object.__setattr__(self, "_n", self._n + 1)
                    return f(*a, **k)
                return wrap

        def make_body(eng_name):
            def body(eng):
                hw = [0] * segs
                hwT = 0
                last = -1
                for si, st in enumerate(steps):
                    se, fn, inc = st[0], st[1], st[2]
                    relax = st[3] if len(st) > 3 else 0
                    if se != eng_name:
                        continue
                    d = si - 1 - relax
                    if d >= 0:
                        if not (last >= 0 and d <= last):
                            pass
                        g_wait(eng, hw, d)
                        if eng_name != "tensor" and cumT[d + 1] > hwT:
                            eng.wait_ge(Tsem, cumT[d + 1])
                            hwT = cumT[d + 1]
                    last = si
                    sem = Tsem if se == "tensor" else Gs[segof[si]]
                    if inc % 16 == 0 and inc > 0:
                        lastins = fn(eng)
                        lastins.then_inc(sem, inc)
                    else:
                        fn(DrainProxy(eng))
                        eng.drain().then_inc(sem, inc)
            return body

        used = {st[0] for st in steps}
        for name in ENGS:
            if name in used:
                getattr(blk, name)(make_body(name))
        @blk.sync
        def _(sync):
            hw = [0] * segs
            g_wait(sync, hw, n - 1)
            if cumT[-1] > 0:
                sync.wait_ge(Tsem, cumT[-1])
        self.es.__exit__(None, None, None)
        return self.nc


def tsc(t, i, w=128):
    return t[:, i * w:(i + 1) * w]



def e_ln(p, x_ap, out_ap, tmp, cola, eps=1e-5):
    def v1(v):
        v.reduce_sum(out=cola[:, 0:1], in_=x_ap, axis=AX, negate=True)
        return v.tensor_scalar_mul(out=cola[:, 0:1], in0=cola[:, 0:1],
                                   scalar1=1.0 / D)
    p.step("vector", v1)
    p.step("scalar", lambda s: s.activation(out=tmp[:], in_=x_ap, func=AF.Square,
                                            bias=cola[:, 0:1]))
    def v2(v):
        v.reduce_sum(out=cola[:, 1:2], in_=tmp[:], axis=AX)
        v.tensor_scalar_mul(out=cola[:, 1:2], in0=cola[:, 1:2], scalar1=1.0 / D)
        return v.tensor_scalar_add(out=cola[:, 1:2], in0=cola[:, 1:2],
                                   scalar1=eps)
    p.step("vector", v2)
    p.step("vector", lambda v: v.reciprocal(out=cola[:, 2:3], in_=cola[:, 1:2]))
    p.step("scalar", lambda s: s.activation(out=cola[:, 2:3], in_=cola[:, 2:3],
                                            func=AF.Sqrt))
    p.step("vector", lambda v: v.tensor_mul(out=cola[:, 3:4], in0=cola[:, 0:1],
                                            in1=cola[:, 2:3]))
    p.step("scalar", lambda s: s.activation(out=out_ap, in_=x_ap, func=AF.Identity,
                                            bias=cola[:, 3:4], scale=cola[:, 2:3]))


def e_quant(p, x_ap, out_bf, tmp_f32, colm, cols, colis):
    def v1(v):
        v.reduce_max(out=colm, in_=x_ap, axis=AX, apply_absolute_value=True)
        v.tensor_scalar_max(out=colm, in0=colm, scalar1=1e-5)
        v.reciprocal(out=cols, in_=colm)
        v.tensor_scalar_mul(out=cols, in0=cols, scalar1=127.0)
        return v.tensor_scalar_mul(out=colis, in0=colm, scalar1=1.0 / 127.0)
    p.step("vector", v1)
    p.step("scalar", lambda s: s.activation(out=tmp_f32, in_=x_ap,
                                            func=AF.Identity,
                                            bias=p._mgc[:, 0:1],
                                            scale=cols))
    def v2(v):
        v.tensor_copy(out=tmp_f32.bitcast(I32), in_=tmp_f32.bitcast(I32))
        return v.tensor_scalar_sub(out=out_bf, in0=tmp_f32, scalar1=MAGIC)
    p.step("vector", v2)


def e_transpose(p, src_bf, dst_fn, ident, ps_ap, n):
    for j in range(n):
        p.step("tensor", lambda t, j=j: t.transpose(
            ps_ap, src_bf[:, j * 128:(j + 1) * 128], ident[:]))
        p.step("vector", lambda v, j=j: v.tensor_copy(out=dst_fn(j), in_=ps_ap))


def e_proj(p, lhsT_sb, w_sb, ps, out_emit, n_chunks=2, nd=ND, ti=None):
    for nch in range(n_chunks):
        def mm(t, nch=nch):
            for kk in range(nd):
                last = t.matmul(ps[:], tsc(lhsT_sb, kk),
                                w_sb[:, kk * n_chunks * 512 + nch * 512:
                                     kk * n_chunks * 512 + nch * 512 + 512],
                                start=(kk == 0), stop=(kk == nd - 1))
            return last
        p.step("tensor", mm)
        out_emit(nch, ps)


def e_softmax_seg(p, psa, psb, out_bf, t1, cola, seg, scale):
    p.step("scalar", lambda s: s.activation(out=t1[:, 0:512], in_=psa[:],
                                            func=AF.Identity, scale=scale))
    p.step("scalar", lambda s: s.activation(out=t1[:, 512:1024], in_=psb[:],
                                            func=AF.Identity, scale=scale))
    p.step("vector", lambda v: v.reduce_max(out=cola[:, 0:1], in_=t1[:], axis=AX,
                                            negate=True))
    p.step("scalar", lambda s: s.activation(out=t1[:], in_=t1[:], func=AF.Exp,
                                            bias=cola[:, 0:1]))
    def v2(v):
        v.reduce_sum(out=seg[:], in_=t1[:].rearrange("p (g k) -> p g k", k=64),
                     axis=AX)
        return v.reciprocal(out=seg[:], in_=seg[:])
    p.step("vector", v2)
    p.step("vector", lambda v: v.tensor_tensor(
        out=out_bf.rearrange("p (g k) -> p g k", k=64),
        in0=t1[:].rearrange("p (g k) -> p g k", k=64),
        in1=seg[:].unsqueeze(2).broadcast_to([128, 16, 64]), op=ALU.mult))


def e_wquant(p, wT_d, wq_sb, ntiles, width, stg, asum, sc1, thr_col, mw_col,
             ones128_f, tmpw, mean_n):
    for j in range(ntiles):
        p.step("sync", lambda s, j=j: s.dma_start(
            out=stg[:, 0:width],
            in_=wT_d[j * 128:(j + 1) * 128, :]), dma=True)
        def v1(v, j=j):
            v.reduce_sum(out=sc1[:, 0:1],
                         in_=stg[:, 0:width],
                         axis=AX, apply_absolute_value=True)
            if j == 0:
                return v.tensor_copy(out=asum[:, 0:1], in_=sc1[:, 0:1])
            return v.tensor_add(out=asum[:, 0:1], in0=asum[:, 0:1],
                                in1=sc1[:, 0:1])
        p.step("vector", v1)
    ps1 = p._ps1
    p.step("tensor", lambda t: t.matmul(ps1[0:1, 0:1], asum[:, 0:1],
                                        ones128_f[:, 0:1], start=True, stop=True))
    def v2(v):
        v.tensor_scalar(out=sc1[0:1, 0:1], in0=ps1[0:1, 0:1],
                        scalar1=1.0 / mean_n, scalar2=1e-5,
                        op0=ALU.mult, op1=ALU.max)
        return v.tensor_scalar_mul(out=sc1[0:1, 1:2], in0=sc1[0:1, 0:1],
                                   scalar1=0.5)
    p.step("vector", v2)
    ones_row = p._ones_row_f
    p.step("tensor", lambda t: t.matmul(ps1[:, 2:4], ones_row[0:1, :],
                                        sc1[0:1, 0:2], start=True, stop=True))
    def v3(v):
        v.tensor_copy(out=mw_col[:, 0:1], in_=ps1[:, 2:3])
        return v.tensor_copy(out=thr_col[:, 0:1], in_=ps1[:, 3:4])
    p.step("vector", v3)
    for j in range(ntiles):
        p.step("sync", lambda s, j=j: s.dma_start(
            out=stg[:, 0:width],
            in_=wT_d[j * 128:(j + 1) * 128, :]), dma=True)
        p.step("scalar", lambda s: s.sign(out=tmpw[:, 0:width],
                                          in_=stg[:, 0:width]))
        p.step("scalar", lambda s: s.activation(out=stg[:, 0:width],
                                                in_=stg[:, 0:width], func=AF.Abs))
        def v4(v, j=j):
            v.tensor_single_scalar(out=stg[:, 0:width],
                                   in_=stg[:, 0:width],
                                   scalar=thr_col[:, 0:1], op=ALU.is_gt)
            v.drain()
            return v.tensor_tensor(out=wq_sb[:, j * width:(j + 1) * width],
                                   in0=tmpw[:, 0:width],
                                   in1=stg[:, 0:width], op=ALU.mult)
        p.step("vector", v4)



def build_l1():
    p = Prog()
    nc = p.nc
    x_d = nc.dram_tensor("x", [T, D], F32, kind="ExternalInput").ap()
    wqT_d = nc.dram_tensor("wqT", [D, D], F32, kind="ExternalInput").ap()
    wkT_d = nc.dram_tensor("wkT", [D, D], F32, kind="ExternalInput").ap()
    wvT_d = nc.dram_tensor("wvT", [D, D], F32, kind="ExternalInput").ap()
    ident_d = nc.dram_tensor("ident", [128, 128], BF16, kind="ExternalInput").ap()
    onesb_d = nc.dram_tensor("onesb", [128, 1], BF16, kind="ExternalInput").ap()
    sq_d = nc.dram_tensor("sq", [T, D], BF16, kind="ExternalOutput").ap()
    sk_d = nc.dram_tensor("sk", [T, D], BF16, kind="ExternalOutput").ap()
    sv_d = nc.dram_tensor("sv", [T, D], BF16, kind="ExternalOutput").ap()
    kvt_d = nc.dram_tensor("kvtot", [64, 1040], F32, kind="ExternalOutput").ap()
    with p.es:
        ident = p.sbuf("s_ident", [128, 128], BF16)
        onesb = p.sbuf("s_onesb", [128, 1], BF16)
        xt = p.sbuf("xt", [128, D], F32)
        xln = p.sbuf("xln", [128, D], F32)
        xlnb = p.sbuf("xlnb", [128, D], BF16)
        xlnT = p.sbuf("xlnT", [128, ND * 128], BF16)
        wq = p.sbuf("wq", [128, ND * D], BF16)
        wk = p.sbuf("wk", [128, ND * D], BF16)
        wv = p.sbuf("wv", [128, ND * D], BF16)
        wstg = p.sbuf("wstg", [128, 2 * D], F32)
        t0 = p.sbuf("t0", [128, D], F32)
        t1 = p.sbuf("t1", [128, D], F32)
        t2 = p.sbuf("t2", [128, D], F32)
        qb = p.sbuf("qb", [128, D], BF16)
        kb = p.sbuf("kb", [128, D], BF16)
        vb = p.sbuf("vb", [128, D], BF16)
        cola = p.sbuf("cola", [128, 4], F32)
        seg = p.sbuf("seg", [128, 16], F32)
        kvtot = p.sbuf("s_kvtot", [64, 1040], F32)
        ps_a = p.psum("ps_a", [128, 512], F32)
        ps_b = p.psum("ps_b", [128, 512], F32)
        ps_t = p.psum("ps_t", [128, 128], BF16)
        ps_ctx = p.psum("ps_ctx", [64, 1024], F32)
        ps_ks = p.psum("ps_ks", [64, 16], F32)

        p.step("sync", lambda s: s.dma_start(out=ident[:], in_=ident_d), dma=True)
        p.step("sync", lambda s: s.dma_start(out=onesb[:], in_=onesb_d), dma=True)
        for wi, (wd, wsb) in enumerate(((wqT_d, wq), (wkT_d, wk), (wvT_d, wv))):
            for j in range(ND):
                sl = (wi * ND + j) % 2
                p.step("sync", lambda s, wd=wd, j=j, sl=sl: s.dma_start(
                    out=wstg[:, sl * D:(sl + 1) * D],
                    in_=wd[j * 128:(j + 1) * 128, :]), dma=True)
                p.step("vector", lambda v, wsb=wsb, j=j, sl=sl: v.tensor_copy(
                    out=wsb[:, j * D:(j + 1) * D],
                    in_=wstg[:, sl * D:(sl + 1) * D]))
        for i in range(NT):
            p.step("sync", lambda s, i=i: s.dma_start(
                out=xt[:], in_=x_d[i * 128:(i + 1) * 128, :]), dma=True)
            e_ln(p, xt[:], xln[:], t0, cola)
            p.step("vector", lambda v: v.tensor_copy(out=xlnb[:], in_=xln[:]))
            e_transpose(p, xlnb, lambda j: tsc(xlnT, j), ident, ps_t[:], ND)
            def qemit(nch, ps):
                pass
            for nch in range(2):
                def mmq(t, nch=nch):
                    psx = ps_a if nch == 0 else ps_b
                    for kk in range(ND):
                        last = t.matmul(psx[:], tsc(xlnT, kk),
                                        wq[:, kk * D + nch * 512:kk * D + nch * 512 + 512],
                                        start=(kk == 0), stop=(kk == ND - 1))
                    return last
                p.step("tensor", mmq)
            e_softmax_seg(p, ps_a, ps_b, qb[:], t1, cola, seg, QS)
            for nch in range(2):
                def mmk(t, nch=nch):
                    psx = ps_a if nch == 0 else ps_b
                    for kk in range(ND):
                        last = t.matmul(psx[:], tsc(xlnT, kk),
                                        wk[:, kk * D + nch * 512:kk * D + nch * 512 + 512],
                                        start=(kk == 0), stop=(kk == ND - 1))
                    return last
                p.step("tensor", mmk)
                psx = ps_a if nch == 0 else ps_b
                p.step("scalar", lambda s, nch=nch, psx=psx: s.activation(
                    out=t0[:, nch * 512:nch * 512 + 512], in_=psx[:],
                    func=AF.Relu, scale=QS))
                p.step("scalar", lambda s, nch=nch, psx=psx: s.activation(
                    out=t1[:, nch * 512:nch * 512 + 512], in_=psx[:],
                    func=AF.Identity, scale=QS))
            p.step("vector", lambda v: v.tensor_sub(out=t2[:], in0=t1[:], in1=t0[:]))
            p.step("scalar", lambda s: s.activation(out=t2[:], in_=t2[:], func=AF.Exp))
            p.step("vector", lambda v: v.tensor_add(out=kb[:], in0=t2[:], in1=t0[:]))
            for nch in range(2):
                def mmv(t, nch=nch):
                    psx = ps_a if nch == 0 else ps_b
                    for kk in range(ND):
                        last = t.matmul(psx[:], tsc(xlnT, kk),
                                        wv[:, kk * D + nch * 512:kk * D + nch * 512 + 512],
                                        start=(kk == 0), stop=(kk == ND - 1))
                    return last
                p.step("tensor", mmv)
                psx = ps_a if nch == 0 else ps_b
                p.step("scalar", lambda s, nch=nch, psx=psx: s.activation(
                    out=vb[:, nch * 512:nch * 512 + 512], in_=psx[:],
                    func=AF.Identity))
            p.step("gpsimd", lambda g, i=i: g.dma_start(
                out=sq_d[i * 128:(i + 1) * 128, :], in_=qb[:]), dma=True)
            p.step("gpsimd", lambda g, i=i: g.dma_start(
                out=sk_d[i * 128:(i + 1) * 128, :], in_=kb[:]), dma=True)
            p.step("gpsimd", lambda g, i=i: g.dma_start(
                out=sv_d[i * 128:(i + 1) * 128, :], in_=vb[:]), dma=True)
            def mmctx(t, i=i):
                for h in range(H):
                    t.matmul(ps_ctx[:, h * 64:(h + 1) * 64],
                             kb[:, h * 64:(h + 1) * 64],
                             vb[:, h * 64:(h + 1) * 64],
                             start=True, stop=True)
                    last = t.matmul(ps_ks[:, h:h + 1],
                                    kb[:, h * 64:(h + 1) * 64],
                                    onesb[:, :],
                                    start=True, stop=True)
                return last
            p.step("tensor", mmctx)
            def vacc(v, i=i):
                if i == 0:
                    v.tensor_copy(out=kvtot[:, 0:1024], in_=ps_ctx[:])
                    return v.tensor_copy(out=kvtot[:, 1024:1040], in_=ps_ks[:])
                v.tensor_tensor(out=kvtot[:, 0:1024], in0=kvtot[:, 0:1024],
                                in1=ps_ctx[:], op=ALU.add)
                return v.tensor_tensor(out=kvtot[:, 1024:1040],
                                       in0=kvtot[:, 1024:1040],
                                       in1=ps_ks[:], op=ALU.add)
            p.step("vector", vacc)
        p.step("sync", lambda s: s.dma_start(out=kvt_d, in_=kvtot[:]), dma=True)
        return p.finish()



def e_ffnpre(p, nc, pre, x1_d, sxqT_d, sis_d, hb_d, w1T_d, names):
    pass


def build_l2():
    p = Prog()
    nc = p.nc
    x_d = nc.dram_tensor("x", [T, D], F32, kind="ExternalInput").ap()
    sq_d = nc.dram_tensor("sq", [T, D], BF16, kind="ExternalInput").ap()
    sk_d = nc.dram_tensor("sk", [T, D], BF16, kind="ExternalInput").ap()
    sv_d = nc.dram_tensor("sv", [T, D], BF16, kind="ExternalInput").ap()
    kvall_d = nc.dram_tensor("kvall", [8, 64, 1040], F32, kind="ExternalInput").ap()
    pmask_d = nc.dram_tensor("pmask", [64, 8], F32, kind="ExternalInput").ap()
    woT_d = nc.dram_tensor("woT", [D, D], F32, kind="ExternalInput").ap()
    borep_d = nc.dram_tensor("borep", [128, D], F32, kind="ExternalInput").ap()
    w1T_d = nc.dram_tensor("w1T", [D, DH4], F32, kind="ExternalInput").ap()
    ident_d = nc.dram_tensor("ident", [128, 128], BF16, kind="ExternalInput").ap()
    onesrf_d = nc.dram_tensor("onesrf", [1, 128], F32, kind="ExternalInput").ap()
    onesf_d = nc.dram_tensor("onesf", [128, 1], F32, kind="ExternalInput").ap()
    onesb_d = nc.dram_tensor("onesb", [128, 1], BF16, kind="ExternalInput").ap()
    x1_d = nc.dram_tensor("x1", [T, D], F32, kind="ExternalOutput").ap()
    sxqT_d = nc.dram_tensor("sxqT", [D, T], BF16, kind="ExternalOutput").ap()
    sis_d = nc.dram_tensor("sis", [T], F32, kind="ExternalOutput").ap()
    hb_d = nc.dram_tensor("hb", [DH4, 2], F32, kind="ExternalOutput").ap()
    with p.es:
        ident = p.sbuf("s_ident", [128, 128], BF16)
        onesrf = p.sbuf("s_onesrf", [1, 128], F32)
        onesf = p.sbuf("s_onesf", [128, 1], F32)
        onesb = p.sbuf("s_onesb", [128, 1], BF16)
        pmask = p.sbuf("s_pmask", [64, 8], F32)
        borep = p.sbuf("s_borep", [128, D], F32)
        kvsb = p.sbuf("kvsb", [64, 8 * 1040], F32)
        running = p.sbuf("running", [128, 1040], F32)
        rk = p.sbuf("rk", [128, 16], F32)
        cnbf = p.sbuf("cnbf", [128, 1024], BF16)
        wo = p.sbuf("wo", [128, ND * D], BF16)
        w1q = p.sbuf("w1q", [128, ND * DH4], BF16)
        wstg = p.sbuf("wstg", [128, DH4], F32)
        tmpw = p.sbuf("tmpw", [128, DH4], F32)
        asum = p.sbuf("asum", [128, 1], F32)
        sc1 = p.sbuf("sc1", [128, 8], F32)
        thr = p.sbuf("thr", [128, 1], F32)
        mw = p.sbuf("mw", [128, 1], F32)
        xt = p.sbuf("xt", [128, D], F32)
        qb = p.sbuf("qb", [128, D], BF16)
        kb = p.sbuf("kb", [128, D], BF16)
        vb = p.sbuf("vb", [128, D], BF16)
        qTt = p.sbuf("qTt", [128, ND * 128], BF16)
        attT = p.sbuf("attT", [128, ND * 128], BF16)
        y0 = p.sbuf("y0", [128, D], F32)
        t0 = p.sbuf("t0", [128, D], F32)
        t1 = p.sbuf("t1", [128, D], F32)
        cola = p.sbuf("cola", [128, 8], F32)
        xq = p.sbuf("xq", [128, D], BF16)
        xqTt = p.sbuf("xqTt", [128, ND * 128], BF16)
        xqcols = p.sbuf("xqcols", [128, 16], BF16)
        bnd = p.sbuf("bnd", [1, 4], F32)
        scl2 = p.sbuf("scl2", [128, 2], F32)
        hb_sb = p.sbuf("hb_sb", [128, 64], F32)
        mgc = p.sbuf("mgc", [128, 1], F32)
        ps_a = p.psum("ps_a", [128, 512], F32)
        ps_at8 = p.psum("ps_at8", [128, 1024], F32)
        ps_cx = p.psum("ps_cx", [128, 1024], F32)
        ps_ks = p.psum("ps_ks", [128, 16], F32)
        ps_t = p.psum("ps_t", [128, 128], BF16)
        ps_m = p.psum("ps_m", [128, 4], F32)
        p._ps1 = ps_m
        p._ones_row_f = onesrf
        p._mgc = mgc
        p.step("vector", lambda v: v.memset(mgc[:], MAGIC))

        for dd, ss in ((ident_d, ident), (onesrf_d, onesrf), (onesf_d, onesf),
                       (onesb_d, onesb), (pmask_d, pmask), (borep_d, borep)):
            p.step("sync", lambda s, dd=dd, ss=ss: s.dma_start(out=ss[:], in_=dd),
                   dma=True)
        p.step("sync", lambda s: s.dma_start(
            out=kvsb[:].rearrange("p (r c) -> p r c", r=8),
            in_=kvall_d.transpose([1, 0, 2])), dma=True)
        for j in range(ND):
            p.step("sync", lambda s, j=j: s.dma_start(
                out=wstg[:, 0:D],
                in_=woT_d[j * 128:(j + 1) * 128, :]), dma=True)
            p.step("vector", lambda v, j=j: v.tensor_copy(
                out=wo[:, j * D:(j + 1) * D], in_=wstg[:, 0:D]))
        def carry(v):
            v.tensor_scalar(out=running[0:64, :], in0=kvsb[:, 0:1040],
                            scalar1=pmask[:, 0:1], scalar2=None, op0=ALU.mult)
            for r in range(1, 8):
                v.drain()
                v.scalar_tensor_tensor(out=running[0:64, :],
                                       in0=kvsb[:, r * 1040:(r + 1) * 1040],
                                       scalar=pmask[:, r:r + 1],
                                       in1=running[0:64, :],
                                       op0=ALU.mult, op1=ALU.add)
            return v.drain()
        p.step("vector", carry)
        p.step("sync", lambda s: s.dma_start(out=running[64:128, :],
                                             in_=running[0:64, :]), dma=True)

        for i in range(NT):
            p.step("sync", lambda s, i=i: s.dma_start(
                out=xt[:], in_=x_d[i * 128:(i + 1) * 128, :]), dma=True)
            p.step("sync", lambda s, i=i: s.dma_start(
                out=qb[:], in_=sq_d[i * 128:(i + 1) * 128, :]), dma=True)
            p.step("sync", lambda s, i=i: s.dma_start(
                out=kb[:], in_=sk_d[i * 128:(i + 1) * 128, :]), dma=True)
            p.step("sync", lambda s, i=i: s.dma_start(
                out=vb[:], in_=sv_d[i * 128:(i + 1) * 128, :]), dma=True)
            e_transpose(p, qb, lambda j: tsc(qTt, j), ident, ps_t[:], ND)
            for b in range(2):
                def norm(v, b=b):
                    v.tensor_scalar_add(out=rk[:], in0=running[:, 1024:1040],
                                        scalar1=1e-6)
                    v.drain()
                    v.reciprocal(out=rk[:], in_=rk[:])
                    v.drain()
                    return v.tensor_tensor(
                        out=cnbf[:].rearrange("p (h e) -> p h e", e=64),
                        in0=running[:, 0:1024].rearrange("p (h e) -> p h e", e=64),
                        in1=rk[:].unsqueeze(2).broadcast_to([128, 16, 64]),
                        op=ALU.mult)
                p.step("vector", norm)
                def attmm(t, b=b):
                    for h in range(H):
                        ob = (h % 2) * 64
                        last = t.matmul(
                            ps_at8[ob:ob + 64,
                                   (h // 2) * 128 + b * 64:(h // 2) * 128 + b * 64 + 64],
                            cnbf[ob:ob + 64, h * 64:(h + 1) * 64],
                            qTt[ob:ob + 64,
                                (h // 2) * 128 + b * 64:(h // 2) * 128 + b * 64 + 64],
                            start=True, stop=True,
                            tile_position=(ob, ob))
                    return last
                p.step("tensor", attmm)
                def ctxmm(t, b=b):
                    for h in range(H):
                        for ob in (0, 64):
                            t.matmul(ps_cx[ob:ob + 64, h * 64:(h + 1) * 64],
                                     kb[b * 64:b * 64 + 64, h * 64:(h + 1) * 64],
                                     vb[b * 64:b * 64 + 64, h * 64:(h + 1) * 64],
                                     start=True, stop=True,
                                     tile_position=(b * 64, ob))
                            last = t.matmul(ps_ks[ob:ob + 64, h:h + 1],
                                            kb[b * 64:b * 64 + 64,
                                               h * 64:(h + 1) * 64],
                                            onesb[b * 64:b * 64 + 64, 0:1],
                                            start=True, stop=True,
                                            tile_position=(b * 64, ob))
                    return last
                p.step("tensor", ctxmm)
                def upd(v):
                    v.tensor_tensor(out=running[:, 0:1024], in0=running[:, 0:1024],
                                    in1=ps_cx[:], op=ALU.add)
                    v.drain()
                    return v.tensor_tensor(out=running[:, 1024:1040],
                                           in0=running[:, 1024:1040],
                                           in1=ps_ks[:], op=ALU.add)
                p.step("vector", upd)
            def evat(v):
                v.tensor_copy(out=attT[:, 0:512], in_=ps_at8[:, 0:512])
                v.drain()
                return v.tensor_copy(out=attT[:, 512:1024], in_=ps_at8[:, 512:1024])
            p.step("vector", evat)
            for nch in range(2):
                def mmo(t, nch=nch):
                    for j in range(ND):
                        last = t.matmul(ps_a[:], tsc(attT, j),
                                        wo[:, j * D + nch * 512:j * D + nch * 512 + 512],
                                        start=(j == 0), stop=(j == ND - 1))
                    return last
                p.step("tensor", mmo)
                p.step("vector", lambda v, nch=nch: v.tensor_tensor(
                    out=y0[:, nch * 512:nch * 512 + 512], in0=ps_a[:],
                    in1=borep[:, nch * 512:nch * 512 + 512], op=ALU.add))
            e_ln(p, y0[:], t1[:], t0, cola)
            p.step("vector", lambda v: v.tensor_add(out=xt[:], in0=xt[:], in1=t1[:]))
            p.step("gpsimd", lambda g, i=i: g.dma_start(
                out=x1_d[i * 128:(i + 1) * 128, :], in_=xt[:]), dma=True)
            e_ln(p, xt[:], t1[:], t0, cola)
            e_quant(p, t1[:], xq[:], t0[:], cola[:, 4:5], cola[:, 5:6], cola[:, 6:7])
            p.step("gpsimd", lambda g, i=i: g.dma_start(
                out=sis_d[i * 128:(i + 1) * 128].unsqueeze(1), in_=cola[:, 6:7]),
                dma=True)
            e_transpose(p, xq, lambda j: tsc(xqTt, j), ident, ps_t[:], ND)
            for j in range(ND):
                p.step("gpsimd", lambda g, i=i, j=j: g.dma_start(
                    out=sxqT_d[j * 128:(j + 1) * 128, i * 128:(i + 1) * 128],
                    in_=tsc(xqTt, j)), dma=True)
            if i == 0:
                def grab0(v):
                    for j in range(ND):
                        v.tensor_copy(out=xqcols[:, j * 2:j * 2 + 1],
                                      in_=xqTt[:, j * 128:j * 128 + 1])
                        v.drain()
                    return v.tensor_copy(out=bnd[0:1, 0:1], in_=cola[0:1, 6:7])
                p.step("vector", grab0)
            if i == NT - 1:
                def grabN(v):
                    for j in range(ND):
                        v.tensor_copy(out=xqcols[:, j * 2 + 1:j * 2 + 2],
                                      in_=xqTt[:, j * 128 + 127:j * 128 + 128])
                        v.drain()
                    return v.drain()
                p.step("vector", grabN)
                p.step("gpsimd", lambda g: g.dma_start(out=bnd[0:1, 1:2],
                                                       in_=cola[127:128, 6:7]),
                       dma=True)
        e_wquant(p, w1T_d, w1q, ND, DH4, wstg, asum, sc1, thr, mw, onesf, tmpw,
                 W1MEAN_N)
        def mkscl(v):
            return v.tensor_tensor(out=bnd[0:1, 2:4], in0=bnd[0:1, 0:2],
                                   in1=mw[0:1, 0:1].broadcast_to([1, 2]),
                                   op=ALU.mult)
        p.step("vector", mkscl)
        p.step("tensor", lambda t: t.matmul(ps_m[:, 0:2], onesrf[0:1, :],
                                            bnd[0:1, 2:4], start=True, stop=True))
        p.step("vector", lambda v: v.tensor_copy(out=scl2[:], in_=ps_m[:, 0:2]))
        for ci in range(NH4):
            def hbmm(t, ci=ci):
                for kk in range(ND):
                    last = t.matmul(ps_m[:, 2:4], w1q[:, kk * DH4 + ci * 128:
                                                      kk * DH4 + ci * 128 + 128],
                                    xqcols[:, kk * 2:kk * 2 + 2],
                                    start=(kk == 0), stop=(kk == ND - 1))
                return last
            p.step("tensor", hbmm)
            p.step("vector", lambda v, ci=ci: v.tensor_tensor(
                out=hb_sb[:, ci * 2:ci * 2 + 2], in0=ps_m[:, 2:4], in1=scl2[:],
                op=ALU.mult))
        p.step("sync", lambda s: s.dma_start(
            out=hb_d.rearrange("(c p) j -> p c j", p=128),
            in_=hb_sb[:].rearrange("p (c j) -> p c j", j=2)), dma=True)
        return p.finish()


def build_ffn_main(do_attn):
    p = Prog()
    nc = p.nc
    xin_d = nc.dram_tensor("xin", [T, D], F32, kind="ExternalInput").ap()
    sxqT_d = nc.dram_tensor("sxqT", [D, T], BF16, kind="ExternalInput").ap()
    sis_d = nc.dram_tensor("sis", [T], F32, kind="ExternalInput").ap()
    hball_d = nc.dram_tensor("hball", [128, 512], F32, kind="ExternalInput").ap()
    hsel_d = nc.dram_tensor("hsel", [128, 32], F32, kind="ExternalInput").ap()
    w1T_d = nc.dram_tensor("w1T", [D, DH4], F32, kind="ExternalInput").ap()
    w2T_d = nc.dram_tensor("w2T", [DH4, D], F32, kind="ExternalInput").ap()
    cw_d = nc.dram_tensor("cw", [128, NH4 * 3], F32, kind="ExternalInput").ap()
    cb_d = nc.dram_tensor("cb", [128, NH4], F32, kind="ExternalInput").ap()
    ident_d = nc.dram_tensor("ident", [128, 128], BF16, kind="ExternalInput").ap()
    identf_d = nc.dram_tensor("identf", [128, 128], F32, kind="ExternalInput").ap()
    onesrf_d = nc.dram_tensor("onesrf", [1, 128], F32, kind="ExternalInput").ap()
    onesf_d = nc.dram_tensor("onesf", [128, 1], F32, kind="ExternalInput").ap()
    xout_d = nc.dram_tensor("xout", [T, D], F32, kind="ExternalOutput").ap()
    if do_attn:
        wkT_d = nc.dram_tensor("wkT", [D, D], F32, kind="ExternalInput").ap()
        wvT_d = nc.dram_tensor("wvT", [D, D], F32, kind="ExternalInput").ap()
        wgT_d = nc.dram_tensor("wgT", [D, D], F32, kind="ExternalInput").ap()
        lq_d = nc.dram_tensor("lq", [128, 8 * 64], BF16, kind="ExternalInput").ap()
        cst_d = nc.dram_tensor("cst", [T, 64], F32, kind="ExternalInput").ap()
        maskT_d = nc.dram_tensor("maskT", [128, 128], F32, kind="ExternalInput").ap()
        sgT_d = nc.dram_tensor("sgT", [D, T], BF16, kind="ExternalOutput").ap()
        satt_d = nc.dram_tensor("satt", [D, 64], F32, kind="ExternalOutput").ap()
    with p.es:
        ident = p.sbuf("s_ident", [128, 128], BF16)
        identf = p.sbuf("s_identf", [128, 128], F32)
        onesrf = p.sbuf("s_onesrf", [1, 128], F32)
        onesf = p.sbuf("s_onesf", [128, 1], F32)
        hsel = p.sbuf("s_hsel", [128, 32], F32)
        cwsb = p.sbuf("s_cw", [128, NH4 * 3], F32)
        cbsb = p.sbuf("s_cb", [128, NH4], F32)
        w1q = p.sbuf("w1q", [128, ND * DH4], BF16)
        w2q = p.sbuf("w2q", [128, NH4 * D], BF16)
        asum = p.sbuf("asum", [128, 1], F32)
        sc1 = p.sbuf("sc1", [128, 8], F32)
        thr1 = p.sbuf("thr1", [128, 1], F32)
        mw1 = p.sbuf("mw1", [128, 1], F32)
        thr2 = p.sbuf("thr2", [128, 1], F32)
        mw2 = p.sbuf("mw2", [128, 1], F32)
        sis_sb = p.sbuf("sis_sb", [1, T], F32)
        halo = p.sbuf("halo", [128, 64], F32)
        stash = p.sbuf("stash", [128, 32], F32)
        xqb = p.sbuf("xqb", [128, ND * 257], BF16)
        srep = p.sbuf("srep", [128, 257], F32)
        hext = p.sbuf("hext", [128, 258], F32)
        conv = p.sbuf("conv", [128, 256], F32)
        hsblk = p.sbuf("hsblk", [128, NH4 * 256], F32)
        tmpw = hsblk[:, 0:DH4]
        wstg = hsblk[:, DH4:2 * DH4]
        gall = hsblk[:, DH4:DH4 + 512]
        mx = p.sbuf("mx", [128, 256], F32)
        mcol = p.sbuf("mcol", [128, 8], F32)
        s2row = p.sbuf("s2row", [1, 256], F32)
        srq = p.sbuf("srq", [128, 256], F32)
        sclc = p.sbuf("sclc", [128, 2], F32)
        tz = p.sbuf("tz", [128, 256], F32)
        hq = p.sbuf("hq", [128, 512], BF16)
        ysb = p.sbuf("ysb", [128, 2 * D], F32)
        xb = p.sbuf("xb", [128, 2 * D], F32)
        t0 = p.sbuf("t0", [128, D], F32)
        t1 = p.sbuf("t1", [128, D], F32)
        cola = p.sbuf("cola", [128, 8], F32)
        ps_h = p.psum("ps_h", [128, 257], F32)
        ps_y = [p.psum(f"ps_y{i}", [128, 512], F32) for i in range(4)]
        ps_f = p.psum("ps_f", [128, 128], F32)
        ps_s = p.psum("ps_s", [128, 257], F32)
        ps_m = ps_s
        ps_tb3 = p.psum("ps_tb3", [128, 128], BF16)
        p._ps1 = ps_s
        p._ones_row_f = onesrf

        for dd, ss in ((ident_d, ident), (identf_d, identf), (onesrf_d, onesrf),
                       (onesf_d, onesf), (hsel_d, hsel), (cw_d, cwsb),
                       (cb_d, cbsb)):
            p.step("sync", lambda s, dd=dd, ss=ss: s.dma_start(out=ss[:], in_=dd),
                   dma=True)
        p.step("sync", lambda s: s.dma_start(out=sis_sb[0:1, :],
                                             in_=sis_d.unsqueeze(0)), dma=True)
        p.step("sync", lambda s: s.dma_start(out=gall[:], in_=hball_d), dma=True)
        def mkhalo(v):
            g4 = gall[:].rearrange("p (c k) -> p c k", k=16)
            v.tensor_tensor(out=tmpw[:, 0:512].rearrange("p (c k) -> p c k", k=16),
                            in0=g4,
                            in1=hsel[:, 0:16].unsqueeze(1).broadcast_to([128, 32, 16]),
                            op=ALU.mult)
            v.drain()
            v.reduce_sum(out=halo[:].rearrange("p (c j) -> p c j", j=2)[:, :, 0],
                         in_=tmpw[:, 0:512].rearrange("p (c k) -> p c k", k=16),
                         axis=AX)
            v.drain()
            v.tensor_tensor(out=tmpw[:, 0:512].rearrange("p (c k) -> p c k", k=16),
                            in0=g4,
                            in1=hsel[:, 16:32].unsqueeze(1).broadcast_to([128, 32, 16]),
                            op=ALU.mult)
            v.drain()
            v.reduce_sum(out=halo[:].rearrange("p (c j) -> p c j", j=2)[:, :, 1],
                         in_=tmpw[:, 0:512].rearrange("p (c k) -> p c k", k=16),
                         axis=AX)
            return v.drain()
        p.step("vector", mkhalo)
        e_wquant(p, w1T_d, w1q, ND, DH4, wstg, asum, sc1, thr1, mw1, onesf, tmpw,
                 W1MEAN_N)
        e_wquant(p, w2T_d, w2q, NH4, D, wstg, asum, sc1, thr2, mw2, onesf, tmpw,
                 W1MEAN_N)

        NBLK = 8
        for b in range(NBLK):
            w = 257 if b < NBLK - 1 else 256
            for j in range(ND):
                p.step("sync", lambda s, b=b, w=w, j=j: s.dma_start(
                    out=xqb[:, j * 257:j * 257 + w],
                    in_=sxqT_d[j * 128:(j + 1) * 128, b * 256:b * 256 + w]),
                    dma=True)
            p.step("tensor", lambda t, b=b, w=w: t.matmul(
                ps_s[:, 0:w], onesrf[0:1, :], sis_sb[0:1, b * 256:b * 256 + w],
                start=True, stop=True))
            p.step("vector", lambda v, w=w: v.tensor_copy(out=srep[:, 0:w],
                                                          in_=ps_s[:, 0:w]))
            for ci in range(NH4):
                def w1mm(t, ci=ci, w=w):
                    for kk in range(ND):
                        last = t.matmul(ps_h[:, 0:w],
                                        w1q[:, kk * DH4 + ci * 128:
                                            kk * DH4 + ci * 128 + 128],
                                        xqb[:, kk * 257:kk * 257 + w],
                                        start=(kk == 0), stop=(kk == ND - 1))
                    return last
                p.step("tensor", w1mm, relax=3)
                def mkhext(v, b=b, ci=ci, w=w):
                    if b == 0:
                        v.tensor_copy(out=hext[:, 0:1],
                                      in_=halo[:, ci * 2:ci * 2 + 1])
                    else:
                        v.tensor_copy(out=hext[:, 0:1], in_=stash[:, ci:ci + 1])
                    v.drain()
                    v.tensor_tensor(out=hext[:, 1:1 + w], in0=ps_h[:, 0:w],
                                    in1=srep[:, 0:w], op=ALU.mult)
                    v.drain()
                    v.tensor_single_scalar(out=hext[:, 1:1 + w],
                                           in_=hext[:, 1:1 + w],
                                           scalar=mw1[:, 0:1], op=ALU.mult)
                    v.drain()
                    if b == NBLK - 1:
                        v.tensor_copy(out=hext[:, 257:258],
                                      in_=halo[:, ci * 2 + 1:ci * 2 + 2])
                        v.drain()
                    v.tensor_copy(out=stash[:, ci:ci + 1], in_=hext[:, 256:257])
                    v.drain()
                    v.tensor_single_scalar(out=conv[:], in_=hext[:, 0:256],
                                           scalar=cwsb[:, ci * 3:ci * 3 + 1],
                                           op=ALU.mult)
                    v.drain()
                    v.scalar_tensor_tensor(out=conv[:], in0=hext[:, 1:257],
                                           scalar=cwsb[:, ci * 3 + 1:ci * 3 + 2],
                                           in1=conv[:], op0=ALU.mult, op1=ALU.add)
                    v.drain()
                    return v.scalar_tensor_tensor(out=conv[:], in0=hext[:, 2:258],
                                                  scalar=cwsb[:, ci * 3 + 2:ci * 3 + 3],
                                                  in1=conv[:], op0=ALU.mult,
                                                  op1=ALU.add)
                p.step("vector", mkhext)
                p.step("scalar", lambda s, ci=ci: s.activation(
                    out=hsblk[:, ci * 256:ci * 256 + 256], in_=conv[:],
                    func=AF.Silu, bias=cbsb[:, ci:ci + 1]))
                p.step("scalar", lambda s, ci=ci: s.activation(
                    out=conv[:], in_=hsblk[:, ci * 256:ci * 256 + 256],
                    func=AF.Abs))
                if ci == 0:
                    p.step("vector", lambda v: v.tensor_copy(out=mx[:],
                                                             in_=conv[:]))
                else:
                    p.step("vector", lambda v: v.tensor_tensor(
                        out=mx[:], in0=mx[:], in1=conv[:], op=ALU.max))
            for ts in range(2):
                p.step("tensor", lambda t, ts=ts: t.transpose(
                    ps_f[:], mx[:, ts * 128:ts * 128 + 128], identf[:]))
                def mk_s(v, ts=ts):
                    v.reduce_max(out=mcol[:, ts:ts + 1], in_=ps_f[:], axis=AX)
                    v.drain()
                    v.tensor_scalar_max(out=mcol[:, ts:ts + 1],
                                        in0=mcol[:, ts:ts + 1], scalar1=1e-5)
                    v.drain()
                    v.reciprocal(out=mcol[:, 2 + ts:3 + ts], in_=mcol[:, ts:ts + 1])
                    v.drain()
                    v.tensor_scalar_mul(out=mcol[:, 2 + ts:3 + ts],
                                        in0=mcol[:, 2 + ts:3 + ts], scalar1=127.0)
                    v.drain()
                    v.tensor_scalar_mul(out=mcol[:, 4 + ts:5 + ts],
                                        in0=mcol[:, ts:ts + 1], scalar1=1.0 / 127.0)
                    v.drain()
                    return v.tensor_single_scalar(out=mcol[:, 6 + ts:7 + ts],
                                                  in_=mcol[:, 4 + ts:5 + ts],
                                                  scalar=mw2[:, 0:1], op=ALU.mult)
                p.step("vector", mk_s)
            for ts in range(2):
                p.step("tensor", lambda t, ts=ts: t.transpose(
                    ps_f[0:1, :], mcol[:, 2 + ts:3 + ts], identf[:]))
                p.step("vector", lambda v, ts=ts: v.tensor_copy(
                    out=s2row[0:1, ts * 128:ts * 128 + 128], in_=ps_f[0:1, :]))
            for ts in range(2):
                p.step("tensor", lambda t, ts=ts: t.matmul(
                    ps_s[:, ts * 128:ts * 128 + 128], onesrf[0:1, :],
                    s2row[0:1, ts * 128:ts * 128 + 128], start=True, stop=True))
            p.step("vector", lambda v: v.tensor_copy(out=srq[:],
                                                     in_=ps_s[:, 0:256]))
            for ci in range(NH4):
                hqo = (ci % 2) * 256
                def qz(v, ci=ci, hqo=hqo):
                    v.tensor_tensor(out=tz[:], in0=hsblk[:, ci * 256:ci * 256 + 256],
                                    in1=srq[:], op=ALU.mult)
                    v.drain()
                    v.tensor_scalar_add(out=tz[:], in0=tz[:], scalar1=MAGIC)
                    v.drain()
                    v.tensor_copy(out=tz[:].bitcast(I32), in_=tz[:].bitcast(I32))
                    v.drain()
                    return v.tensor_scalar_sub(out=hq[:, hqo:hqo + 256],
                                               in0=tz[:], scalar1=MAGIC)
                p.step("vector", qz, relax=(1 if ci > 0 else 0))
                def w2mm(t, ci=ci, hqo=hqo):
                    for ts in range(2):
                        for dc in range(2):
                            last = t.matmul(ps_y[ts * 2 + dc][:],
                                            hq[:, hqo + ts * 128:hqo + ts * 128 + 128],
                                            w2q[:, ci * D + dc * 512:
                                                ci * D + dc * 512 + 512],
                                            start=(ci == 0), stop=(ci == NH4 - 1))
                    return last
                p.step("tensor", w2mm)
            p.step("sync", lambda s, b=b: s.dma_start(
                out=xb[:].rearrange("p (s d) -> p s d", s=2),
                in_=xin_d[b * 256:b * 256 + 256, :].rearrange(
                    "(s p) d -> p s d", p=128)), dma=True)
            for ts in range(2):
                for dc in range(2):
                    p.step("scalar", lambda s, ts=ts, dc=dc: s.activation(
                        out=ysb[:, ts * D + dc * 512:ts * D + dc * 512 + 512],
                        in_=ps_y[ts * 2 + dc][:], func=AF.Identity,
                        scale=mcol[:, 6 + ts:7 + ts]))
            for ts in range(2):
                e_ln(p, ysb[:, ts * D:(ts + 1) * D], t1[:], t0, cola)
                p.step("vector", lambda v, ts=ts: v.tensor_tensor(
                    out=xb[:, ts * D:(ts + 1) * D], in0=xb[:, ts * D:(ts + 1) * D],
                    in1=t1[:], op=ALU.add))
            p.step("gpsimd", lambda g, b=b: g.dma_start(
                out=xout_d[b * 256:b * 256 + 256, :].rearrange(
                    "(s p) d -> p s d", p=128),
                in_=xb[:].rearrange("p (s d) -> p s d", s=2)), dma=True)

        if do_attn:
            wk = w1q[:, 0:ND * D]
            wv = w1q[:, ND * D:2 * ND * D]
            wg = w1q[:, 2 * ND * D:3 * ND * D]
            xlnT = w2q[:, 0:1024]
            kTt = w2q[:, 1024:2048]
            gTt = w2q[:, 2048:3072]
            lq = w2q[:, 3072:3584]
            vb2 = w2q[:, 3584:4608]
            kbf = w2q[:, 4608:5632]
            xlnb = w2q[:, 5632:6656]
            pbf = w2q[:, 6656:6784]
            pTb = w2q[:, 6784:6912]
            gb = w2q[:, 6912:7936]
            kf = hsblk[:, 0:1024]
            krot = hsblk[:, 1024:2048]
            scsb = hsblk[:, 2048:2176]
            att = hsblk[:, 2176:2688]
            cs_sb = hsblk[:, 2688:2752]
            maskT = hsblk[:, 2752:2880]
            xt = hsblk[:, 2880:3904]
            seg = p.sbuf("seg", [128, 16], F32)
            ps_a = ps_y[0]
            ps_b = ps_y[1]
            ps_sc = ps_y[2]
            ps_at = ps_y[3]
            ps_tb = ps_tb3[:]
            for dd, ss in ((lq_d, lq), (maskT_d, maskT)):
                p.step("sync", lambda s, dd=dd, ss=ss: s.dma_start(
                    out=ss[:], in_=dd), dma=True)
            for wd, wsb in ((wkT_d, wk), (wvT_d, wv), (wgT_d, wg)):
                for j in range(ND):
                    p.step("sync", lambda s, wd=wd, j=j: s.dma_start(
                        out=wstg[:, 0:D],
                        in_=wd[j * 128:(j + 1) * 128, :]), dma=True)
                    p.step("vector", lambda v, wsb=wsb, j=j: v.tensor_copy(
                        out=wsb[:, j * D:(j + 1) * D],
                        in_=wstg[:, 0:D]))
            for i in range(NT):
                p.step("sync", lambda s, i=i: s.dma_start(
                    out=xt[:], in_=xout_d[i * 128:(i + 1) * 128, :]), dma=True)
                p.step("sync", lambda s, i=i: s.dma_start(
                    out=cs_sb[:], in_=cst_d[i * 128:(i + 1) * 128, :]), dma=True)
                e_ln(p, xt[:], t1[:], t0, cola)
                p.step("vector", lambda v: v.tensor_copy(out=xlnb[:], in_=t1[:]))
                e_transpose(p, xlnb, lambda j: tsc(xlnT, j), ident, ps_tb, ND)
                for nch in range(2):
                    def mmk(t, nch=nch):
                        psx = ps_a if nch == 0 else ps_b
                        for kk in range(ND):
                            last = t.matmul(psx[:], tsc(xlnT, kk),
                                            wk[:, kk * D + nch * 512:
                                               kk * D + nch * 512 + 512],
                                            start=(kk == 0), stop=(kk == ND - 1))
                        return last
                    p.step("tensor", mmk)
                    psx = ps_a if nch == 0 else ps_b
                    p.step("scalar", lambda s, nch=nch, psx=psx: s.activation(
                        out=kf[:, nch * 512:nch * 512 + 512], in_=psx[:],
                        func=AF.Identity))
                def rope(v):
                    k4 = kf[:].rearrange("p (h d two) -> p two h d", h=16, d=32)
                    o4 = krot[:].rearrange("p (h d two) -> p two h d", h=16, d=32)
                    cos3 = cs_sb[:, 0:32].unsqueeze(1).broadcast_to([128, 16, 32])
                    sin3 = cs_sb[:, 32:64].unsqueeze(1).broadcast_to([128, 16, 32])
                    v.tensor_tensor(out=o4[:, 0], in0=k4[:, 0], in1=cos3,
                                    op=ALU.mult)
                    v.drain()
                    v.tensor_tensor(out=t0[:, 0:512].rearrange(
                        "p (h d) -> p h d", d=32), in0=k4[:, 1], in1=sin3,
                        op=ALU.mult)
                    v.drain()
                    v.tensor_tensor(out=o4[:, 0], in0=o4[:, 0],
                                    in1=t0[:, 0:512].rearrange(
                                        "p (h d) -> p h d", d=32),
                                    op=ALU.subtract)
                    v.drain()
                    v.tensor_tensor(out=o4[:, 1], in0=k4[:, 0], in1=sin3,
                                    op=ALU.mult)
                    v.drain()
                    v.tensor_tensor(out=t0[:, 0:512].rearrange(
                        "p (h d) -> p h d", d=32), in0=k4[:, 1], in1=cos3,
                        op=ALU.mult)
                    v.drain()
                    v.tensor_tensor(out=o4[:, 1], in0=o4[:, 1],
                                    in1=t0[:, 0:512].rearrange(
                                        "p (h d) -> p h d", d=32),
                                    op=ALU.add)
                    v.drain()
                    return v.tensor_copy(out=kbf[:], in_=krot[:])
                p.step("vector", rope)
                e_transpose(p, kbf, lambda j: tsc(kTt, j), ident, ps_tb, ND)
                for nch in range(2):
                    def mmv(t, nch=nch):
                        psx = ps_a if nch == 0 else ps_b
                        for kk in range(ND):
                            last = t.matmul(psx[:], tsc(xlnT, kk),
                                            wv[:, kk * D + nch * 512:
                                               kk * D + nch * 512 + 512],
                                            start=(kk == 0), stop=(kk == ND - 1))
                        return last
                    p.step("tensor", mmv)
                    psx = ps_a if nch == 0 else ps_b
                    p.step("scalar", lambda s, nch=nch, psx=psx: s.activation(
                        out=vb2[:, nch * 512:nch * 512 + 512], in_=psx[:],
                        func=AF.Identity))
                for nch in range(2):
                    def mmg(t, nch=nch):
                        psx = ps_a if nch == 0 else ps_b
                        for kk in range(ND):
                            last = t.matmul(psx[:], tsc(xlnT, kk),
                                            wg[:, kk * D + nch * 512:
                                               kk * D + nch * 512 + 512],
                                            start=(kk == 0), stop=(kk == ND - 1))
                        return last
                    p.step("tensor", mmg)
                e_softmax_seg(p, ps_a, ps_b, gb[:], t1, cola, seg, L1SC)
                e_transpose(p, gb, lambda j: tsc(gTt, j), ident, ps_tb, ND)
                for j in range(ND):
                    p.step("gpsimd", lambda g, i=i, j=j: g.dma_start(
                        out=sgT_d[j * 128:(j + 1) * 128, i * 128:(i + 1) * 128],
                        in_=tsc(gTt, j)), dma=True)
                for hp in range(8):
                    def scmm(t, hp=hp):
                        for hs in range(2):
                            last = t.matmul(
                                ps_sc[hs * 64:hs * 64 + 64, 0:128],
                                lq[hs * 64:hs * 64 + 64, hp * 64:hp * 64 + 64],
                                kTt[hs * 64:hs * 64 + 64,
                                    hp * 128:hp * 128 + 128],
                                start=True, stop=True,
                                tile_position=(hs * 64, hs * 64))
                        return last
                    p.step("tensor", scmm, relax=(7 if hp > 0 else 0))
                    p.step("scalar", lambda s: s.activation(
                        out=scsb[:], in_=ps_sc[:, 0:128], func=AF.Identity,
                        scale=L1SC))
                    def smax(v):
                        v.tensor_tensor(out=scsb[:], in0=scsb[:], in1=maskT[:],
                                        op=ALU.add)
                        v.drain()
                        return v.reduce_max(out=cola[:, 0:1], in_=scsb[:],
                                            axis=AX, negate=True)
                    p.step("vector", smax)
                    p.step("scalar", lambda s: s.activation(
                        out=scsb[:], in_=scsb[:], func=AF.Exp, bias=cola[:, 0:1]))
                    def sdiv(v):
                        v.reduce_sum(out=cola[:, 1:2], in_=scsb[:], axis=AX)
                        v.drain()
                        v.reciprocal(out=cola[:, 1:2], in_=cola[:, 1:2])
                        v.drain()
                        return v.tensor_single_scalar(out=pbf[:], in_=scsb[:],
                                                      scalar=cola[:, 1:2],
                                                      op=ALU.mult)
                    p.step("vector", sdiv)
                    p.step("tensor", lambda t: t.transpose(ps_tb, pbf[:],
                                                           ident[:]))
                    p.step("vector", lambda v: v.tensor_copy(out=pTb[:],
                                                             in_=ps_tb))
                    def atmm(t, hp=hp):
                        for hs in range(2):
                            last = t.matmul(
                                ps_at[hs * 64:hs * 64 + 64, 0:64],
                                vb2[:, (2 * hp + hs) * 64:(2 * hp + hs) * 64 + 64],
                                pTb[:, hs * 64:hs * 64 + 64],
                                start=True, stop=True,
                                tile_position=(0, hs * 64))
                        return last
                    p.step("tensor", atmm)
                    def atacc(v, i=i, hp=hp):
                        if i == 0:
                            return v.tensor_copy(out=att[:, hp * 64:hp * 64 + 64],
                                                 in_=ps_at[:, 0:64])
                        return v.tensor_tensor(out=att[:, hp * 64:hp * 64 + 64],
                                               in0=att[:, hp * 64:hp * 64 + 64],
                                               in1=ps_at[:, 0:64], op=ALU.add)
                    p.step("vector", atacc)
            p.step("sync", lambda s: s.dma_start(
                out=satt_d.rearrange("(j p) l -> p j l", p=128),
                in_=att[:].rearrange("p (j l) -> p j l", l=64)), dma=True)
        return p.finish()


def build_l4():
    p = Prog()
    nc = p.nc
    x2_d = nc.dram_tensor("x2", [T, D], F32, kind="ExternalInput").ap()
    sattall_d = nc.dram_tensor("sattall", [8, D, 64], F32, kind="ExternalInput").ap()
    sgT_d = nc.dram_tensor("sgT", [D, T], BF16, kind="ExternalInput").ap()
    wpT_d = nc.dram_tensor("wpT", [D, D], F32, kind="ExternalInput").ap()
    w1T_d = nc.dram_tensor("w1T", [D, DH4], F32, kind="ExternalInput").ap()
    ident_d = nc.dram_tensor("ident", [128, 128], BF16, kind="ExternalInput").ap()
    onesrf_d = nc.dram_tensor("onesrf", [1, 128], F32, kind="ExternalInput").ap()
    onesf_d = nc.dram_tensor("onesf", [128, 1], F32, kind="ExternalInput").ap()
    x3_d = nc.dram_tensor("x3", [T, D], F32, kind="ExternalOutput").ap()
    sxqT_d = nc.dram_tensor("sxqT", [D, T], BF16, kind="ExternalOutput").ap()
    sis_d = nc.dram_tensor("sis", [T], F32, kind="ExternalOutput").ap()
    hb_d = nc.dram_tensor("hb", [DH4, 2], F32, kind="ExternalOutput").ap()
    with p.es:
        ident = p.sbuf("s_ident", [128, 128], BF16)
        onesrf = p.sbuf("s_onesrf", [1, 128], F32)
        onesf = p.sbuf("s_onesf", [128, 1], F32)
        att8 = p.sbuf("att8", [128, 8 * 512], F32)
        atsum = p.sbuf("atsum", [128, 512], F32)
        atbf = p.sbuf("atbf", [128, 512], BF16)
        aT = p.sbuf("aT", [128, 16 * 64], BF16)
        wp = p.sbuf("wp", [128, ND * D], BF16)
        w1q = p.sbuf("w1q", [128, ND * DH4], BF16)
        wstg = p.sbuf("wstg", [128, DH4], F32)
        tmpw = p.sbuf("tmpw", [128, DH4], F32)
        asum = p.sbuf("asum", [128, 1], F32)
        sc1 = p.sbuf("sc1", [128, 8], F32)
        thr = p.sbuf("thr", [128, 1], F32)
        mw = p.sbuf("mw", [128, 1], F32)
        gTt = p.sbuf("gTt", [128, ND * 128], BF16)
        wTt = p.sbuf("wTt", [128, ND * 128], BF16)
        xt = p.sbuf("xt", [128, D], F32)
        y0 = p.sbuf("y0", [128, D], F32)
        t0 = p.sbuf("t0", [128, D], F32)
        t1 = p.sbuf("t1", [128, D], F32)
        cola = p.sbuf("cola", [128, 8], F32)
        xq = p.sbuf("xq", [128, D], BF16)
        xqTt = p.sbuf("xqTt", [128, ND * 128], BF16)
        xqcols = p.sbuf("xqcols", [128, 16], BF16)
        bnd = p.sbuf("bnd", [1, 4], F32)
        scl2 = p.sbuf("scl2", [128, 2], F32)
        hb_sb = p.sbuf("hb_sb", [128, 64], F32)
        mgc = p.sbuf("mgc", [128, 1], F32)
        ps_a = p.psum("ps_a", [128, 512], F32)
        ps_w = p.psum("ps_w", [128, 128], F32)
        ps_t = p.psum("ps_t", [128, 128], BF16)
        ps_tb2 = p.psum("ps_tb2", [64, 64], BF16)
        ps_m = p.psum("ps_m", [128, 4], F32)
        p._ps1 = ps_m
        p._ones_row_f = onesrf
        p._mgc = mgc
        p.step("vector", lambda v: v.memset(mgc[:], MAGIC))

        for dd, ss in ((ident_d, ident), (onesrf_d, onesrf), (onesf_d, onesf)):
            p.step("sync", lambda s, dd=dd, ss=ss: s.dma_start(out=ss[:], in_=dd),
                   dma=True)
        for r in range(8):
            p.step("sync", lambda s, r=r: s.dma_start(
                out=att8[:, r * 512:(r + 1) * 512].rearrange(
                    "p (j l) -> p j l", l=64),
                in_=sattall_d[r].rearrange("(j p) l -> p j l", p=128)), dma=True)
        def asum_f(v):
            v.tensor_copy(out=atsum[:], in_=att8[:, 0:512])
            for r in range(1, 8):
                v.drain()
                v.tensor_tensor(out=atsum[:], in0=atsum[:],
                                in1=att8[:, r * 512:(r + 1) * 512], op=ALU.add)
            v.drain()
            return v.tensor_copy(out=atbf[:], in_=atsum[:])
        p.step("vector", asum_f)
        for h in range(H):
            j, off = h // 2, (h % 2) * 64
            p.step("tensor", lambda t, j=j, off=off: t.transpose(
                ps_tb2[:], atbf[off:off + 64, j * 64:j * 64 + 64],
                ident[off:off + 64, off:off + 64]))
            p.step("vector", lambda v, h=h: v.tensor_copy(
                out=aT[0:64, h * 64:h * 64 + 64], in_=ps_tb2[:]))
        p.step("sync", lambda s: s.dma_start(out=aT[64:128, :],
                                             in_=aT[0:64, :]), dma=True)
        for j in range(ND):
            p.step("sync", lambda s, j=j: s.dma_start(
                out=wstg[:, 0:D], in_=wpT_d[j * 128:(j + 1) * 128, :]), dma=True)
            p.step("vector", lambda v, j=j: v.tensor_copy(
                out=wp[:, j * D:(j + 1) * D], in_=wstg[:, 0:D]))
        for i in range(NT):
            for j in range(ND):
                p.step("sync", lambda s, i=i, j=j: s.dma_start(
                    out=tsc(gTt, j),
                    in_=sgT_d[j * 128:(j + 1) * 128, i * 128:(i + 1) * 128]),
                    dma=True)
            p.step("sync", lambda s, i=i: s.dma_start(
                out=xt[:], in_=x2_d[i * 128:(i + 1) * 128, :]), dma=True)
            for hp in range(8):
                def wmm(t, hp=hp):
                    for hs in range(2):
                        h = 2 * hp + hs
                        ob = hs * 64
                        last = t.matmul(ps_w[ob:ob + 64, :],
                                        aT[ob:ob + 64, h * 64:h * 64 + 64],
                                        gTt[ob:ob + 64,
                                            hp * 128:hp * 128 + 128],
                                        start=True, stop=True,
                                        tile_position=(ob, ob))
                    return last
                p.step("tensor", wmm)
                p.step("vector", lambda v, hp=hp: v.tensor_copy(
                    out=tsc(wTt, hp), in_=ps_w[:]))
            for nch in range(2):
                def mmp(t, nch=nch):
                    for j in range(ND):
                        last = t.matmul(ps_a[:], tsc(wTt, j),
                                        wp[:, j * D + nch * 512:j * D + nch * 512 + 512],
                                        start=(j == 0), stop=(j == ND - 1))
                    return last
                p.step("tensor", mmp)
                p.step("scalar", lambda s, nch=nch: s.activation(
                    out=y0[:, nch * 512:nch * 512 + 512], in_=ps_a[:],
                    func=AF.Identity))
            e_ln(p, y0[:], t1[:], t0, cola)
            p.step("vector", lambda v: v.tensor_add(out=xt[:], in0=xt[:], in1=t1[:]))
            p.step("gpsimd", lambda g, i=i: g.dma_start(
                out=x3_d[i * 128:(i + 1) * 128, :], in_=xt[:]), dma=True)
            e_ln(p, xt[:], t1[:], t0, cola)
            e_quant(p, t1[:], xq[:], t0[:], cola[:, 4:5], cola[:, 5:6], cola[:, 6:7])
            p.step("gpsimd", lambda g, i=i: g.dma_start(
                out=sis_d[i * 128:(i + 1) * 128].unsqueeze(1), in_=cola[:, 6:7]),
                dma=True)
            e_transpose(p, xq, lambda j: tsc(xqTt, j), ident, ps_t[:], ND)
            for j in range(ND):
                p.step("gpsimd", lambda g, i=i, j=j: g.dma_start(
                    out=sxqT_d[j * 128:(j + 1) * 128, i * 128:(i + 1) * 128],
                    in_=tsc(xqTt, j)), dma=True)
            if i == 0:
                def grab0(v):
                    for j in range(ND):
                        v.tensor_copy(out=xqcols[:, j * 2:j * 2 + 1],
                                      in_=xqTt[:, j * 128:j * 128 + 1])
                        v.drain()
                    return v.tensor_copy(out=bnd[0:1, 0:1], in_=cola[0:1, 6:7])
                p.step("vector", grab0)
            if i == NT - 1:
                def grabN(v):
                    for j in range(ND):
                        v.tensor_copy(out=xqcols[:, j * 2 + 1:j * 2 + 2],
                                      in_=xqTt[:, j * 128 + 127:j * 128 + 128])
                        v.drain()
                    return v.drain()
                p.step("vector", grabN)
                p.step("gpsimd", lambda g: g.dma_start(out=bnd[0:1, 1:2],
                                                       in_=cola[127:128, 6:7]),
                       dma=True)
        e_wquant(p, w1T_d, w1q, ND, DH4, wstg, asum, sc1, thr, mw, onesf, tmpw,
                 W1MEAN_N)
        def mkscl(v):
            return v.tensor_tensor(out=bnd[0:1, 2:4], in0=bnd[0:1, 0:2],
                                   in1=mw[0:1, 0:1].broadcast_to([1, 2]),
                                   op=ALU.mult)
        p.step("vector", mkscl)
        p.step("tensor", lambda t: t.matmul(ps_m[:, 0:2], onesrf[0:1, :],
                                            bnd[0:1, 2:4], start=True, stop=True))
        p.step("vector", lambda v: v.tensor_copy(out=scl2[:], in_=ps_m[:, 0:2]))
        for ci in range(NH4):
            def hbmm(t, ci=ci):
                for kk in range(ND):
                    last = t.matmul(ps_m[:, 2:4], w1q[:, kk * DH4 + ci * 128:
                                                      kk * DH4 + ci * 128 + 128],
                                    xqcols[:, kk * 2:kk * 2 + 2],
                                    start=(kk == 0), stop=(kk == ND - 1))
                return last
            p.step("tensor", hbmm)
            p.step("vector", lambda v, ci=ci: v.tensor_tensor(
                out=hb_sb[:, ci * 2:ci * 2 + 2], in0=ps_m[:, 2:4], in1=scl2[:],
                op=ALU.mult))
        p.step("sync", lambda s: s.dma_start(
            out=hb_d.rearrange("(c p) j -> p c j", p=128),
            in_=hb_sb[:].rearrange("p (c j) -> p c j", j=2)), dma=True)
        return p.finish()



def _host_consts():
    f32 = np.float32
    ident_bf = np.eye(128, dtype=f32).astype(NPBF)
    identf = np.eye(128, dtype=f32)
    onesb = np.ones((128, 1), f32).astype(NPBF)
    onesf = np.ones((128, 1), f32)
    onesrf = np.ones((1, 128), f32)
    return ident_bf, identf, onesb, onesf, onesrf


def kernel(**inputs):
    f32 = np.float32
    xfull = np.ascontiguousarray(
        np.asarray(inputs["L0_attn_x"], f32).reshape(16384, D))
    shards = [np.ascontiguousarray(xfull[c * T:(c + 1) * T]) for c in range(NCORE)]
    tr = lambda w: np.ascontiguousarray(np.asarray(w, f32).T)
    ident_bf, identf, onesb, onesf, onesrf = _host_consts()
    cores = list(range(NCORE))

    nc1 = build_l1()
    im1 = [{"x": shards[c], "wqT": tr(inputs["L0_attn_Wq"]),
            "wkT": tr(inputs["L0_attn_Wk"]), "wvT": tr(inputs["L0_attn_Wv"]),
            "ident": ident_bf, "onesb": onesb} for c in cores]
    r1 = run_bass_kernel_spmd(nc1, im1, core_ids=cores).results

    kvall = np.stack([np.asarray(r1[c]["kvtot"], f32) for c in cores])
    pmasks = [np.broadcast_to(
        (np.arange(8, dtype=f32) < c).astype(f32)[None, :], (64, 8)).copy()
        for c in cores]
    borep = np.broadcast_to(np.asarray(inputs["L0_attn_bo"], f32)[None, :],
                            (128, D)).copy()

    nc2 = build_l2()
    im2 = [{"x": shards[c], "sq": np.asarray(r1[c]["sq"]),
            "sk": np.asarray(r1[c]["sk"]), "sv": np.asarray(r1[c]["sv"]),
            "kvall": kvall, "pmask": pmasks[c], "woT": tr(inputs["L0_attn_Wo"]),
            "borep": borep, "w1T": tr(inputs["L0_ffn_w1"]), "ident": ident_bf,
            "onesrf": onesrf, "onesf": onesf, "onesb": onesb} for c in cores]
    r2 = run_bass_kernel_spmd(nc2, im2, core_ids=cores).results

    hball = np.stack([np.asarray(r2[c]["hb"], f32) for c in cores])
    hball = np.ascontiguousarray(hball.reshape(8, 32, 128, 2)
                                 .transpose(2, 1, 0, 3).reshape(128, 512))
    def hsel_core(c):
        sel = np.zeros((128, 32), f32)
        if c > 0:
            sel[:, (c - 1) * 2 + 1] = 1.0
        if c < 7:
            sel[:, 16 + (c + 1) * 2] = 1.0
        return sel
    cw = np.asarray(inputs["L0_ffn_conv_w"], f32)[:, 0, :]
    cw_sb = np.ascontiguousarray(cw.reshape(NH4, 128, 3).transpose(1, 0, 2)
                                 .reshape(128, NH4 * 3))
    cb = np.asarray(inputs["L0_ffn_conv_b"], f32)
    cb_sb = np.ascontiguousarray(cb.reshape(NH4, 128).T)
    dd = np.arange(32, dtype=np.float64)
    inv = 1.0 / (10000.0 ** (2 * dd / 64.0))
    lqs = np.asarray(inputs["L1_attn_latent_queries"], f32)[0]
    lq_sb = np.zeros((128, 8 * 64), f32)
    for h in range(16):
        blk_ = lqs[:, h, :].T
        lq_sb[(h % 2) * 64:(h % 2) * 64 + 64, (h // 2) * 64:(h // 2) * 64 + 64] = blk_
    maskT = np.where(np.arange(128)[None, :] <= np.arange(64)[:, None],
                     0.0, -1e30).astype(f32)
    maskT2 = np.concatenate([maskT, maskT], axis=0)

    nc3 = build_ffn_main(True)
    im3 = []
    for c in cores:
        tpos = np.arange(c * T, (c + 1) * T, dtype=np.float64)
        ang = tpos[:, None] * inv[None, :]
        cst = np.concatenate([np.cos(ang), np.sin(ang)], 1).astype(f32)
        im3.append({"xin": np.asarray(r2[c]["x1"], f32),
                    "sxqT": np.asarray(r2[c]["sxqT"]),
                    "sis": np.asarray(r2[c]["sis"], f32), "hball": hball,
                    "hsel": hsel_core(c), "w1T": tr(inputs["L0_ffn_w1"]),
                    "w2T": tr(inputs["L0_ffn_w2"]), "cw": cw_sb, "cb": cb_sb,
                    "ident": ident_bf, "identf": identf, "onesrf": onesrf,
                    "onesf": onesf, "wkT": tr(inputs["L1_attn_W_k"]),
                    "wvT": tr(inputs["L1_attn_W_v"]),
                    "wgT": tr(inputs["L1_attn_W_g"]),
                    "lq": lq_sb.astype(NPBF), "cst": cst, "maskT": maskT2})
    r3 = run_bass_kernel_spmd(nc3, im3, core_ids=cores).results

    sattall = np.stack([np.asarray(r3[c]["satt"], f32) for c in cores])
    nc4 = build_l4()
    im4 = [{"x2": np.asarray(r3[c]["xout"], f32), "sattall": sattall,
            "sgT": np.asarray(r3[c]["sgT"]), "wpT": tr(inputs["L1_attn_W_p"]),
            "w1T": tr(inputs["L1_ffn_w1"]), "ident": ident_bf,
            "onesrf": onesrf, "onesf": onesf} for c in cores]
    r4 = run_bass_kernel_spmd(nc4, im4, core_ids=cores).results

    hball1 = np.stack([np.asarray(r4[c]["hb"], f32) for c in cores])
    hball1 = np.ascontiguousarray(hball1.reshape(8, 32, 128, 2)
                                  .transpose(2, 1, 0, 3).reshape(128, 512))
    cw1 = np.asarray(inputs["L1_ffn_conv_w"], f32)[:, 0, :]
    cw1_sb = np.ascontiguousarray(cw1.reshape(NH4, 128, 3).transpose(1, 0, 2)
                                  .reshape(128, NH4 * 3))
    cb1 = np.asarray(inputs["L1_ffn_conv_b"], f32)
    cb1_sb = np.ascontiguousarray(cb1.reshape(NH4, 128).T)
    nc5 = build_ffn_main(False)
    im5 = [{"xin": np.asarray(r4[c]["x3"], f32),
            "sxqT": np.asarray(r4[c]["sxqT"]),
            "sis": np.asarray(r4[c]["sis"], f32), "hball": hball1,
            "hsel": hsel_core(c), "w1T": tr(inputs["L1_ffn_w1"]),
            "w2T": tr(inputs["L1_ffn_w2"]), "cw": cw1_sb, "cb": cb1_sb,
            "ident": ident_bf, "identf": identf, "onesrf": onesrf,
            "onesf": onesf} for c in cores]
    r5 = run_bass_kernel_spmd(nc5, im5, core_ids=cores).results

    out = np.concatenate([np.asarray(r5[c]["xout"], f32) for c in cores], 0)
    return out.reshape(1, 16384, D)
